# version 13
# baseline (speedup 1.0000x reference)
import math
import numpy as np

import concourse.bass as bass
import concourse.mybir as mybir
from concourse.bass_utils import run_bass_kernel_spmd

F32 = mybir.dt.float32
BF16 = mybir.dt.bfloat16
I32 = mybir.dt.int32
AF = mybir.ActivationFunctionType
ALU = mybir.AluOpType


class Cfg:
    def __init__(self, S=8192, D=2048, F=8192, H=16, n_cores=8, splits=None,
                 mlph=1024):
        self.S, self.D, self.F, self.H = S, D, F, H
        self.DH = 128
        self.R = 16
        self.n_cores = n_cores
        self.CB = D // 128
        self.FB = F // 128
        self.QG = 512 if S >= 4096 else 256
        self.SG = self.QG
        self.CK = 128
        assert self.H * self.DH == D and self.CB == self.H
        if splits is None:
            splits = [S // n_cores] * n_cores
        assert sum(splits) == S and all(sp % self.QG == 0 for sp in splits)
        self.splits = splits
        self.starts = [sum(splits[:i]) for i in range(n_cores)]
        self.NMAX = max(splits)
        self.MLPH = min(mlph, self.NMAX)
        self.EPS_LN = 1e-5
        self.EPS_RMS = 1e-6
        self.EPS_GN = 1e-5
        self.MOD_SCALE = 0.1


def gammas(H):
    return 1.0 - np.exp2(-5.0 - np.arange(H, dtype=np.float64))


def host_tables(cfg):
    S, D, H, DH, CK, SG, QG = (cfg.S, cfg.D, cfg.H, cfg.DH, cfg.CK, cfg.SG,
                               cfg.QG)
    g = gammas(H)
    t = {}
    inv = 1.0 / (10000.0 ** (np.arange(0, DH, 2, dtype=np.float64) / DH))
    ang = np.arange(S, dtype=np.float64)[:, None] * inv[None, :]
    cosv, sinv = np.cos(ang), np.sin(ang)
    C = np.empty((S, D), np.float64)
    Sg = np.empty((S, D), np.float64)
    for h in range(H):
        C[:, h * DH:h * DH + 64] = cosv
        C[:, h * DH + 64:h * DH + 128] = cosv
        Sg[:, h * DH:h * DH + 64] = -sinv
        Sg[:, h * DH + 64:h * DH + 128] = sinv
    t["ropeC"] = C.astype(np.float32)
    t["ropeS"] = Sg.astype(np.float32)
    dec = np.empty((S, D), np.float64)
    e = (SG - 1 - (np.arange(S) % SG)).astype(np.float64)
    for h in range(H):
        dec[:, h * DH:(h + 1) * DH] = (g[h] ** e)[:, None]
    t["ropeCd"] = (C * dec).astype(np.float32)
    t["ropeSd"] = (Sg * dec).astype(np.float32)
    kd = np.empty((128, D), np.float64)
    for h in range(H):
        kd[:, h * DH:(h + 1) * DH] = (
            g[h] ** (CK - 1 - np.arange(128, dtype=np.float64)))[:, None]
    t["kdec128"] = kd.astype(np.float32)
    cc = np.empty((128, H * CK), np.float64)
    for h in range(H):
        cc[:, h * CK:(h + 1) * CK] = (
            g[h] ** (np.arange(CK, dtype=np.float64) + 1.0))[None, :]
    t["crossC"] = cc.astype(np.float32)
    dm = np.zeros((CK, H * CK), np.float64)
    idx = np.arange(CK, dtype=np.float64)
    rel = idx[None, :] - idx[:, None]
    for h in range(H):
        dm[:, h * CK:(h + 1) * CK] = np.where(
            rel >= 0, g[h] ** np.maximum(rel, 0.0), 0.0)
    t["dmatT"] = dm.astype(np.float32)
    nmv = QG // 128
    mv = np.zeros((nmv, 128, QG), np.float32)
    for r in range(nmv):
        srow = r * 128 + np.arange(128)[:, None]
        mv[r] = np.where(srow <= np.arange(QG)[None, :], 0.0, -1e30)
    t["masks"] = np.ascontiguousarray(
        mv.transpose(1, 0, 2).reshape(128, nmv * QG))
    t["ident"] = np.eye(128, dtype=np.float32)
    t["gSG"] = g ** SG
    t["gCK"] = g ** CK
    return t




class _Dummy:
    def __getattr__(self, name):
        def _f(*a, **k):
            return None
        return _f


class _Rec:
    def __init__(self, eng, env):
        object.__setattr__(self, "_eng", eng)
        object.__setattr__(self, "_env", env)

    def __getattr__(self, name):
        m = getattr(self._eng, name)

        def _f(*a, **k):
            r = m(*a, **k)
            if r is not None and hasattr(r, "then_inc"):
                self._env.last = r
            return r
        return _f


class Env:
    def __init__(self, mk, active, core):
        self.mk = mk
        self.active = active
        self.core = core
        self.cnt = mk.counters[core]
        self.last = None
        nc = mk.nc
        self._real = {"t": nc.tensor, "v": nc.vector, "s": nc.scalar,
                      "g": nc.gpsimd, "y": nc.sync}
        for e in "tvsgy":
            setattr(self, e.upper(),
                    _Rec(self._real[e], self) if active == e else _Dummy())

    NROT = 4
    ENG = {"c": "g", "b": "g"}

    def dma(self, ename, dst, src):
        rk = "rot_" + ename
        j = self.cnt.get(rk, 0) % self.NROT
        self.cnt[rk] = self.cnt.get(rk, 0) + 1
        key = f"dma_{ename}{j}"
        prev = self.cnt.get(key, 0)
        el = self.ENG.get(ename, ename)
        eng = self._real[el]
        if self.active == el:
            if prev > 0:
                eng.wait_ge(self.mk.sem[key], prev)
            self.last = eng.dma_start(dst, src)
        self._pending_dma_key = key

    def done(self, ename, n=1):
        if ename in ("g", "y", "c", "b") and n == 16:
            key = self._pending_dma_key
        else:
            key = "done_" + ename
        self.cnt[key] = self.cnt.get(key, 0) + n
        if self.active == self.ENG.get(ename, ename):
            assert self.last is not None, f"done({ename}) with no inst"
            self.last.then_inc(self.mk.sem[key], n)
            self.last = None

    def wait(self, consumer, producer, at=None):
        if at is not None:
            if self.active == consumer and at > 0:
                self._real[consumer].wait_ge(self.mk.sem["done_" + producer],
                                             at)
            return
        keys = ["done_" + producer]
        if producer in ("g", "y", "c", "b"):
            keys += [f"dma_{producer}{j}" for j in range(self.NROT)]
        wkey = "w_" + consumer
        seen = self.cnt.setdefault(wkey, {}) if False else None
        for key in keys:
            val = self.cnt.get(key, 0)
            lastk = f"lw_{consumer}_{key}"
            if self.cnt.get(lastk, -1) == val:
                continue
            self.cnt[lastk] = val
            if self.active == consumer and val > 0:
                self._real[consumer].wait_ge(self.mk.sem[key], val)

    def count(self, ename):
        return self.cnt.get("done_" + ename, 0)


class MK:
    def __init__(self, cfg):
        self.cfg = cfg
        self.nc = bass.Bass("TRN2", target_bir_lowering=False)
        self.t = {}
        self.sb = {}
        self.ps = {}
        self.sem = {}
        self.exits = []
        self.counters = [dict() for _ in range(cfg.n_cores)]
        self.cid_regs = {}

    def dram(self, name, shape, dtype, kind="ExternalInput"):
        self.t[name] = self.nc.dram_tensor(name, list(shape), dtype, kind=kind)

    def sbuf(self, name, shape, dtype):
        self.sb[name] = self.nc.alloc_sbuf_tensor("sb_" + name, list(shape), dtype)

    def psum(self, name, shape, dtype=F32):
        self.ps[name] = self.nc.alloc_psum_tensor(name, list(shape), dtype)

    def semaphore(self, name):
        cm = self.nc.semaphore(name)
        s = cm.__enter__()
        self.exits.append(cm)
        self.sem[name] = s
        return s


def emit_phase(mk, name, fn, shared=False, locals_=None, aliases=None):
    mk.phase_marks = getattr(mk, "phase_marks", [])
    mk.phase_marks.append((name, mk.nc.n_instructions()))
    """Emit one Block.  fn(env) is executed once per engine pass; counters
    are snapshot/restored so each pass sees identical counts, and the final
    state (all passes produce identical deltas) is kept.  locals_ maps
    sbuf-name -> (shape, dtype) allocated for this phase only; aliases maps
    name -> existing local name (same tensor)."""
    import contextlib
    nc = mk.nc
    ncores = mk.cfg.n_cores
    stack = contextlib.ExitStack()
    added = []
    for nm, (shape, dtype) in (locals_ or {}).items():
        h = stack.enter_context(
            nc.sbuf_tensor(f"L{name}_{nm}", list(shape), dtype))
        mk.sb[nm] = h
        added.append(nm)
    for nm, src_nm in (aliases or {}).items():
        mk.sb[nm] = mk.sb[src_nm]
        added.append(nm)
    with nc.Block(name) as b:
        entry = [dict(d) for d in mk.counters]
        final = [None] * ncores

        def runner(ename):
            def _run(_eng):
                for c in range(ncores if not shared else 1):
                    mk.counters[c].clear()
                    mk.counters[c].update(entry[c])
                if shared:
                    fn(Env(mk, ename, 0))
                else:
                    reg = mk.cid_regs[ename]
                    for c in range(ncores):
                        with _eng.If_eq(reg, c):
                            fn(Env(mk, ename, c))
                for c in range(ncores if not shared else 1):
                    if final[c] is None:
                        final[c] = dict(mk.counters[c])
                    else:
                        assert final[c] == mk.counters[c], \
                            f"engine passes disagree in {name} core {c}"
            return _run
        b.tensor(runner("t"))
        b.vector(runner("v"))
        b.scalar(runner("s"))
        b.gpsimd(runner("g"))
        b.sync(runner("y"))
        if shared:
            d0 = mk.counters[0]
            for c in range(1, ncores):
                dc = mk.counters[c]
                keys = set(d0) | set(entry[0])
                for k in keys:
                    delta = d0.get(k, 0) - entry[0].get(k, 0)
                    if delta:
                        dc[k] = dc.get(k, 0) + delta
    stack.close()
    for nm in added:
        mk.sb.pop(nm, None)


def build(cfg, upto=99):
    mk = MK(cfg)
    nc = mk.nc
    S, D, F, H, R = cfg.S, cfg.D, cfg.F, cfg.H, cfg.R
    CB, FB, QG, SG, CK, DH = cfg.CB, cfg.FB, cfg.QG, cfg.SG, cfg.CK, cfg.DH
    NMAX, MLPH = cfg.NMAX, cfg.MLPH
    NMV = QG // 128
    NTB = QG // 128
    NDG = D // 512 if D >= 512 else 1
    DG = min(512, D)
    tabs = host_tables(cfg)
    gSG, gCK = tabs["gSG"], tabs["gCK"]
    ISQ = 1.0 / math.sqrt(DH)

    mk.dram("x", (S, D), F32)
    mk.dram("xT", (D, S), F32)
    for n in ("gWq", "gWk", "gWv", "gWg", "gWo", "fWqT", "fWkT", "fWv",
              "fWo"):
        mk.dram(n, (D, D), F32)
    mk.dram("gnw", (128, 1), F32)
    mk.dram("Amat", (D, R), F32)
    mk.dram("Bmat", (R, F), F32)
    mk.dram("WinT", (D, F), F32)
    mk.dram("WupT", (D, F), F32)
    mk.dram("WdnT", (F, D), F32)
    mk.dram("ln1f", (128, CB), F32)
    mk.dram("ln2f", (128, CB), F32)
    mk.dram("cid", (1, 1), I32)
    mk.dram("cvec", (128, 16), F32)
    for n in ("ropeC", "ropeS", "ropeCd", "ropeSd"):
        mk.dram(n, (S, D), F32)
    mk.dram("kdec128", (128, D), F32)
    mk.dram("crossC", (128, H * CK), F32)
    mk.dram("dmatT", (CK, H * CK), F32)
    mk.dram("masks", (128, NMV * QG), F32)
    mk.dram("ident", (128, 128), F32)
    mk.dram("out", (NMAX, D), F32, kind="ExternalOutput")
    NB = S // 128
    NBO = NMAX // 128
    mk.dram("Ksc", (H, NB, 128, 128), BF16, kind="Internal")
    mk.dram("VMsc", (NB, 128, H * 48), BF16, kind="Internal")
    mk.dram("Qsc", (H, NBO, 128, 128), BF16, kind="Internal")
    mk.dram("GTsc", (H, NBO, 128, 128), BF16, kind="Internal")
    mk.dram("SGsc", (H, NBO, 128, 128), BF16, kind="Internal")
    for n in ("gWqB", "gWkB", "gWvB", "gWgB", "fWkTB", "fWqTB"):
        mk.dram(n, (D, D), BF16, kind="Internal")
    mk.dram("WinTB", (D, F), BF16, kind="Internal")
    mk.dram("WupTB", (D, F), BF16, kind="Internal")
    mk.dram("WdnTB", (F, D), BF16, kind="Internal")
    mk.dram("gWoB", (D, D), BF16, kind="Internal")
    mk.dram("rowsD", (3, S), F32, kind="Internal")

    sbf = mk.sbuf
    sbf("cid", (1, 1), I32)
    sbf("cvec", (128, 16), F32)
    sbf("ident", (128, 128), BF16)
    sbf("onesbf", (128, 16), BF16)
    sbf("onesf", (128, 1), F32)
    sbf("onesfr", (1, 128), F32)
    sbf("ln1f", (128, CB), F32)
    sbf("ln2f", (128, CB), F32)
    sbf("gnw", (128, 1), F32)
    sbf("r_col", (128, NBO), F32)
    sbf("nmh_col", (128, NBO), F32)
    sbf("A_bf", (128, CB * R), BF16)
    sbf("knorm", (1, 16), F32)
    sbf("qnorm", (1, 16 * 16), F32)
    sbf("srow", (1, QG), F32)
    sbf("rows3", (1, 3 * QG), F32)
    sbf("modacc", (R, NMAX), F32)
    sbf("btile", (16, 2 * 128), BF16)
    sbf("bias", (128, 1), F32)
    RW = min(D, 512)

    for n in ("pa", "pb", "pc", "pd", "pe", "pf"):
        mk.psum(n, (128, 512))
    mk.psum("pt", (128, 1024), BF16)
    ps = mk.ps

    for e in "tvsgy":
        mk.semaphore("done_" + e)
    for j in range(Env.NROT):
        mk.semaphore(f"dma_g{j}")
        mk.semaphore(f"dma_y{j}")
        mk.semaphore(f"dma_c{j}")
        mk.semaphore(f"dma_b{j}")

    CONSTS = [cfg.EPS_LN, cfg.EPS_RMS, cfg.EPS_GN, 1.0, -1.0,
              1.0 / D, 1.0 / DH]
    for i, v in enumerate(CONSTS):
        nc.const_aps.register(F32, float(v), mk.sb["cvec"][:, i:i + 1])

    sb, t = mk.sb, mk.t

    def n_own(c): return cfg.splits[c]
    def start(c): return cfg.starts[c]
    def end(c): return cfg.starts[c] + cfg.splits[c]

    with nc.Block("ph0") as b:
        @b.gpsimd
        def _(g):
            sld = mk.semaphore("ld0")
            sld2 = mk.semaphore("ld0b")
            n = [0]

            def ld(dst, src):
                g.dma_start(dst, src).then_inc(sld, 16)
                n[0] += 16
            ld(sb["cid"][:, :], t["cid"].ap())
            ld(sb["cvec"][:, :], t["cvec"].ap())
            ld(sb["ident"][:, :], t["ident"].ap())
            ld(sb["ln1f"][:, :], t["ln1f"].ap())
            ld(sb["ln2f"][:, :], t["ln2f"].ap())
            ld(sb["gnw"][:, :], t["gnw"].ap())
            g.wait_ge(sld, n[0])
            g.memset(sb["onesbf"][:, :], 1.0)
            g.memset(sb["onesf"][:, :], 1.0)
            g.memset(sb["onesfr"][:, :], 1.0)
            g.memset(sb["knorm"][:, :], 0.0)
            g.memset(sb["qnorm"][:, :], 0.0)
            g.memset(sb["modacc"][:, :], 0.0).then_inc(sld2, 1)
            g.wait_ge(sld2, 1)

    with nc.Block("ph0b") as b:
        def mkload(ename):
            def _f(eng):
                r = eng.alloc_register("cid_" + ename)
                eng.reg_load(r, sb["cid"][:1, :1])
                mk.cid_regs[ename] = r
            return _f
        b.tensor(mkload("t"))
        b.vector(mkload("v"))
        b.scalar(mkload("s"))
        b.gpsimd(mkload("g"))
        b.sync(mkload("y"))

    def ph_wcast(env):
        def cast(q, dst, src):
            key = f"dma_{q}0"
            env.cnt[key] = env.cnt.get(key, 0) + 16
            if env.active == "g":
                env._real["g"].dma_start(dst, src).then_inc(mk.sem[key], 16)
        for src, dst in (("gWq", "gWqB"), ("gWk", "gWkB"), ("gWv", "gWvB"),
                         ("gWg", "gWgB"), ("fWkT", "fWkTB"),
                         ("fWqT", "fWqTB")):
            cast("c", t[dst].ap(), t[src].ap())
        for src2, dst2 in (("WinT", "WinTB"), ("WupT", "WupTB"),
                           ("WdnT", "WdnTB")):
            rows = mk.t[src2].shape[0]
            for i in range(4):
                rsl = slice(i * rows // 4, (i + 1) * rows // 4)
                cast("b", t[dst2].ap()[rsl, :], t[src2].ap()[rsl, :])
        cast("b", t["gWoB"].ap(), t["gWo"].ap())
    if upto >= 1:
        emit_phase(mk, "ph1", ph_wcast, shared=True)

    def ph_stats(env):
        c = env.core
        E = end(c)
        for g0 in range(0, E, QG):
            env.wait("g", "g")
            env.wait("g", "s")
            env.wait("g", "t")
            env.dma("g", sb["xf"][:, :].rearrange("p (c q) -> p c q", c=CB),
                    t["xT"].ap().rearrange("(c p) s -> p c s",
                                           p=128)[:, :, g0:g0 + QG])
            env.done("g", 16)
            env.wait("s", "g")
            env.wait("s", "t")
            for cb in range(CB):
                env.S.activation(sb["sqf"][:, cb * QG:(cb + 1) * QG],
                                 sb["xf"][:, cb * QG:(cb + 1) * QG],
                                 AF.Square)
            env.done("s")
            env.wait("t", "s")
            env.wait("t", "v")
            env.wait("t", "y")
            for cb in range(CB):
                sl = slice(cb * QG, (cb + 1) * QG)
                env.T.matmul(ps["pa"][0:1, 0:QG], sb["onesbf"][:, 0:1],
                             sb["xf"][:, sl], start=(cb == 0),
                             stop=(cb == CB - 1))
            for cb in range(CB):
                sl = slice(cb * QG, (cb + 1) * QG)
                env.T.matmul(ps["pb"][0:1, 0:QG], sb["onesbf"][:, 0:1],
                             sb["sqf"][:, sl], start=(cb == 0),
                             stop=(cb == CB - 1))
            env.done("t")
            env.wait("v", "t")
            env.wait("v", "y")
            env.V.tensor_scalar_mul(sb["rwM"][0:1, :], ps["pa"][0:1, 0:QG],
                                    1.0 / D)
            env.V.tensor_scalar_mul(sb["rwS"][0:1, :], ps["pb"][0:1, 0:QG],
                                    1.0 / D)
            env.done("v")
            env.wait("v", "v")
            env.V.scalar_tensor_tensor(sb["rwV"][0:1, :], sb["rwM"][0:1, :],
                                       -1.0, sb["rwM"][0:1, :],
                                       ALU.mult, ALU.mult)
            env.done("v")
            env.wait("v", "v")
            env.V.tensor_add(sb["rwV"][0:1, :], sb["rwV"][0:1, :],
                             sb["rwS"][0:1, :])
            env.done("v")
            env.wait("s", "v")
            env.S.activation(sb["rwR"][0:1, :], sb["rwV"][0:1, :], AF.Ln,
                             bias=cfg.EPS_LN)
            env.done("s")
            env.wait("s", "s")
            env.S.activation(sb["rwR"][0:1, :], sb["rwR"][0:1, :], AF.Exp,
                             scale=-0.5)
            env.done("s")
            env.wait("v", "s")
            env.V.tensor_tensor(sb["rwH"][0:1, :], sb["rwM"][0:1, :],
                                sb["rwR"][0:1, :], ALU.mult)
            env.V.tensor_tensor(sb["rwS"][0:1, :], sb["rwR"][0:1, :],
                                sb["rwR"][0:1, :], ALU.mult)
            env.done("v")
            env.wait("v", "v")
            env.V.tensor_tensor(sb["rwS"][0:1, :], sb["rwS"][0:1, :],
                                sb["rwV"][0:1, :], ALU.mult)
            env.done("v")
            env.wait("s", "v")
            env.S.activation(sb["rwS"][0:1, :], sb["rwS"][0:1, :], AF.Ln,
                             bias=cfg.EPS_RMS)
            env.done("s")
            env.wait("s", "s")
            env.S.activation(sb["rwS"][0:1, :], sb["rwS"][0:1, :], AF.Exp,
                             scale=-0.5)
            env.done("s")
            env.wait("y", "s")
            env.wait("y", "v")
            env.dma("y", t["rowsD"].ap()[0:1, g0:g0 + QG], sb["rwR"][0:1, :])
            env.done("y", 16)
            env.dma("y", t["rowsD"].ap()[1:2, g0:g0 + QG], sb["rwH"][0:1, :])
            env.done("y", 16)
            env.dma("y", t["rowsD"].ap()[2:3, g0:g0 + QG], sb["rwS"][0:1, :])
            env.done("y", 16)
            if g0 >= start(c):
                env.wait("t", "s")
                env.wait("t", "v")
                for tb in range(QG // 128):
                    rs2 = slice(tb * 128, (tb + 1) * 128)
                    env.T.matmul(ps["pc"][0:128, tb * 2:tb * 2 + 1],
                                 sb["rwR"][0:1, rs2], sb["onesf"][0:1, 0:1],
                                 start=True, stop=True)
                    env.T.matmul(ps["pc"][0:128, tb * 2 + 1:tb * 2 + 2],
                                 sb["rwH"][0:1, rs2], sb["onesf"][0:1, 0:1],
                                 start=True, stop=True)
                env.done("t")
                env.wait("v", "t")
                for tb in range(QG // 128):
                    b0 = (g0 - start(c)) // 128 + tb
                    env.V.tensor_copy(sb["r_col"][:, b0:b0 + 1],
                                      ps["pc"][0:128, tb * 2:tb * 2 + 1])
                    env.V.tensor_scalar_mul(
                        sb["nmh_col"][:, b0:b0 + 1],
                        ps["pc"][0:128, tb * 2 + 1:tb * 2 + 2], -1.0)
                env.done("v")
                env.wait("t", "v")
        env.wait("g", "c")
    def ph_wvm(env):
        for cb in range(CB):
            env.dma("g", sb["A_bf"][:, cb * R:(cb + 1) * R],
                            t["Amat"].ap()[cb * 128:(cb + 1) * 128, :])
            env.done("g", 16)
        dbt = {}
        for db in range(2 * CB):
            slot = db % 2
            w0 = slot * CB * 128
            if db >= 2:
                env.wait("g", "t", at=dbt[db - 2])
            wn, col = (("fWo", db) if db < CB else ("fWv", db - CB))
            env.dma("g",
                    sb["wtile"][:, w0:w0 + CB * 128].rearrange(
                        "p (c n) -> p c n", n=128),
                    t[wn].ap().rearrange(
                        "(c p) d -> p c d",
                        p=128)[:, :, col * 128:(col + 1) * 128])
            env.done("g", 16)
            env.wait("t", "g")
            env.wait("t", "v")
            if db < CB:
                for cb in range(CB):
                    env.T.matmul(ps["pa"][0:128, 0:R],
                                 sb["wtile"][:, w0 + cb * 128:
                                             w0 + (cb + 1) * 128],
                                 sb["A_bf"][:, cb * R:(cb + 1) * R],
                                 start=(cb == 0), stop=(cb == CB - 1))
                env.done("t")
                dbt[db] = env.count("t")
                env.wait("v", "t")
                env.V.tensor_copy(sb["Mh"][:, db * R:(db + 1) * R],
                                  ps["pa"][0:128, 0:R])
                env.done("v")
            else:
                cbx = db - CB
                for h in range(H):
                    env.T.matmul(ps["pa"][0:128, h * R:(h + 1) * R],
                                 sb["wtile"][:, w0 + h * 128:
                                             w0 + (h + 1) * 128],
                                 sb["Mh"][:, h * R:(h + 1) * R],
                                 start=True, stop=True)
                env.done("t")
                dbt[db] = env.count("t")
                env.wait("v", "t")
                env.V.tensor_copy(sb["WVMr"][:, cbx * H * R:
                                             (cbx + 1) * H * R],
                                  ps["pa"][0:128, 0:H * R])
                env.done("v")
        env.wait("t", "v")
    import contextlib
    wvm_stack = contextlib.ExitStack()
    mk.sb["WVMr"] = wvm_stack.enter_context(
        nc.sbuf_tensor("sb_WVMr", [128, CB * H * R], BF16))
    if upto >= 3:
        emit_phase(mk, "ph3", ph_wvm, shared=True,
                   locals_={"wtile": ((128, 2 * CB * 128), BF16),
                            "Mh": ((128, CB * R), BF16)})
    if upto >= 2:
        emit_phase(mk, "ph2", ph_stats, locals_={
            "xf": ((128, CB * QG), BF16),
            "sqf": ((128, CB * QG), BF16),
            "rwM": ((1, QG), F32), "rwV": ((1, QG), F32),
            "rwR": ((1, QG), F32), "rwH": ((1, QG), F32),
            "rwS": ((1, QG), F32)})

    def load_xf_group(env, g0):
        env.wait("g", "g")
        env.dma("g", sb["xf"][:, :].rearrange("p (c q) -> p c q", c=CB),
                t["xT"].ap().rearrange("(c p) s -> p c s",
                                       p=128)[:, :, g0:g0 + QG])
        env.done("g", 16)

    def build_bcasts(env, g0, rr=False):
        env.wait("g", "t")
        env.wait("g", "v")
        env.wait("g", "y")
        nload = 3 if rr else 2
        for j in range(nload):
            env.dma("g", sb["rows3"][0:1, j * QG:(j + 1) * QG],
                    t["rowsD"].ap()[j:j + 1, g0:g0 + QG])
            env.done("g", 16)
        names = [(0, "rbc"), (1, "mbc")]
        if rr:
            names.append((2, "rrbc"))
        env.wait("t", "g")
        for j, bcn in names:
            env.wait("t", "v")
            env.T.matmul(ps["pf"][0:128, 0:QG], sb["onesfr"][0:1, :],
                         sb["rows3"][0:1, j * QG:(j + 1) * QG],
                         start=True, stop=True)
            env.done("t")
            env.wait("v", "t")
            env.wait("v", "g")
            env.wait("v", "s")
            env.V.tensor_copy(sb[bcn][:, :], ps["pf"][0:128, 0:QG])
            env.done("v")
        env.wait("t", "v")

    def build_xn(env, with_x1=False):
        env.wait("g", "v")
        env.wait("g", "t")
        env.wait("g", "s")
        for cb in range(CB):
            sl = slice(cb * QG, (cb + 1) * QG)
            env.G.tensor_tensor(sb["xf"][:, sl], sb["xf"][:, sl],
                                sb["rbc"][:, :], ALU.mult)
        env.done("g")
        env.wait("v", "g")
        env.wait("v", "t")
        for cb in range(CB):
            sl = slice(cb * QG, (cb + 1) * QG)
            env.V.tensor_tensor(sb["xnf"][:, sl], sb["xf"][:, sl],
                                sb["mbc"][:, :], ALU.subtract)
        env.done("v")
        if with_x1:
            env.wait("v", "v")
            for cb in range(CB):
                sl = slice(cb * QG, (cb + 1) * QG)
                env.V.scalar_tensor_tensor(sb["x1f"][:, sl], sb["xnf"][:, sl],
                                           sb["ln1f"][:, cb:cb + 1],
                                           sb["rrbc"][:, :],
                                           ALU.mult, ALU.mult)
            env.done("v")

    def proj_token_major(env, wname, dstb, extra_wait=()):
        banks = ("pa", "pb", "pc", "pd")[:NTB]
        SLC = min(8, CB)
        NSL = CB // SLC
        wsrc = t[wname].ap().rearrange("(c p) d -> p c d", p=128)
        tcnt = {}
        slab_i = 0
        for dg in range(D // DG):
            for sl in range(NSL):
                par = slab_i % 2
                if slab_i >= 2:
                    env.wait("g", "t", at=tcnt[slab_i - 2])
                else:
                    env.wait("g", "t")
                env.dma("g",
                        sb["wrhs"][:, par * SLC * DG:
                                   (par + 1) * SLC * DG].rearrange(
                            "p (c d) -> p c d", d=DG),
                        wsrc[:, sl * SLC:(sl + 1) * SLC,
                             dg * DG:(dg + 1) * DG])
                env.done("g", 16)
                env.wait("t", "g")
                if slab_i == 0:
                    env.wait("t", "s")
                for c2 in range(SLC):
                    cb2 = sl * SLC + c2
                    for tb in range(NTB):
                        env.T.matmul(
                            ps[banks[tb]][0:128, 0:DG],
                            sb["xnf"][:, cb2 * QG + tb * 128:
                                      cb2 * QG + (tb + 1) * 128],
                            sb["wrhs"][:, (par * SLC + c2) * DG:
                                       (par * SLC + c2 + 1) * DG],
                            start=(cb2 == 0), stop=(cb2 == CB - 1))
                env.done("t")
                tcnt[slab_i] = env.count("t")
                slab_i += 1
            env.wait("s", "t")
            env.wait("s", "g")
            env.wait("s", "v")
            for tb in range(NTB):
                env.S.activation(
                    sb[dstb][:, tb * D + dg * DG:tb * D + (dg + 1) * DG],
                    ps[banks[tb]][0:128, 0:DG], AF.Identity)
            env.done("s")
            env.wait("t", "s")


    def rope_blk(env, srcb, tb, p0, cn, sn, dst_slot):
        for hw_ in range(D // RW):
            off = tb * D + hw_ * RW
            env.wait("g", "g")
            env.wait("g", "v")
            env.dma("g", sb["tta"][:, :],
                    t[cn].ap()[p0:p0 + 128, hw_ * RW:(hw_ + 1) * RW])
            env.done("g", 16)
            env.dma("g", sb["ttb"][:, :],
                    t[sn].ap()[p0:p0 + 128, hw_ * RW:(hw_ + 1) * RW])
            env.done("g", 16)
            env.wait("g", "s")
            env.wait("g", "g")
            env.G.tensor_tensor(sb["rpa"][:, :],
                                sb[srcb][:, off:off + RW],
                                sb["tta"][:, :],
                                ALU.mult)
            env.done("g")
            sap = sb[srcb][:, :]
            swap = bass.AP(sap.tensor, sap.offset + off + 64,
                           [[sap.tensor.shape[1], 128],
                            [128, RW // 128], [-64, 2], [1, 64]])
            env.wait("v", "s")
            env.wait("v", "g")
            env.V.tensor_tensor(sb["rpb"][:, :], swap, sb["ttb"][:, :],
                                ALU.mult)
            env.done("v")
            env.wait("v", "g")
            env.wait("v", "v")
            env.wait("v", "t")
            env.wait("v", "y")
            dst = (sb[dst_slot][:, hw_ * RW:(hw_ + 1) * RW]
                   if dst_slot == "evb"
                   else sb[dst_slot][:, off:off + RW])
            env.V.tensor_tensor(dst, sb["rpa"][:, :], sb["rpb"][:, :],
                                ALU.add)
            env.done("v")

    def transpose_heads(env, srcap_fn, dstb, tb):
        for h in range(H):
            env.T.transpose(ps["pt"][0:128, (h % 8) * 128:(h % 8) * 128 + 128],
                            srcap_fn(h), sb["ident"][:, :])
            if h % 8 == 7 or h == H - 1:
                env.done("t")
                env.wait("s", "t")
                lo = h - (h % 8)
                for hh in range(lo, h + 1):
                    env.S.activation(
                        sb[dstb][:, hh * QG + tb * 128:
                                 hh * QG + (tb + 1) * 128],
                        ps["pt"][0:128, (hh % 8) * 128:(hh % 8) * 128 + 128],
                        AF.Identity)
                env.done("s")
                env.wait("t", "s")

    def ph_gla(env):
        c = env.core
        E, S0 = end(c), start(c)
        env.G.memset(sb["S_st"][:, :], 0.0)
        env.done("g")
        env.dma("g", sb["kdec128"][:, :], t["kdec128"].ap())
        env.done("g", 16)
        env.dma("g", sb["crossC"][:, :], t["crossC"].ap())
        env.done("g", 16)
        env.dma("g", sb["dmatT"][0:CK, :], t["dmatT"].ap())
        env.done("g", 16)
        xbufs = (mk.sb["xf"], mk.sb["xf2"])

        def bind(gi):
            mk.sb["xf"] = xbufs[gi % 2]
            mk.sb["xnf"] = xbufs[gi % 2]

        def prep(gi, g0):
            bind(gi)
            load_xf_group(env, g0)
            build_bcasts(env, g0)
            build_xn(env)

        groups = list(range(0, E, SG))
        prep(0, 0)
        for gi, g0 in enumerate(groups):
            own = g0 >= S0
            bind(gi)
            env.wait("t", "v")
            proj_token_major(env, "gWkB", "kraw")
            proj_token_major(env, "gWvB", "vtb")
            if gi + 1 < len(groups):
                prep(gi + 1, groups[gi + 1])
                bind(gi)
            for tb in range(NTB):
                p0 = g0 + tb * 128
                cn, sn = ("ropeC", "ropeS") if own else ("ropeCd", "ropeSd")
                if own:
                    rope_blk(env, "kraw", tb, p0, cn, sn, "evb")
                    env.wait("v", "v")
                    env.V.tensor_tensor(sb["ktb"][:, tb * D:(tb + 1) * D],
                                        sb["evb"][:, 0:D], sb["kdec128"][:, :],
                                        ALU.mult)
                    env.done("v")
                    env.wait("t", "v")
                    transpose_heads(env, lambda h: sb["evb"][:, h * 128:
                                                            (h + 1) * 128],
                                    "kfb", tb)
                else:
                    rope_blk(env, "kraw", tb, p0, cn, sn, "ktb")
            if own:
                proj_token_major(env, "gWqB", "kraw")
                for tb in range(NTB):
                    p0 = g0 + tb * 128
                    rope_blk(env, "kraw", tb, p0, "ropeC", "ropeS", "evb")
                    env.wait("t", "v")
                    transpose_heads(env, lambda h: sb["evb"][:, h * 128:
                                                            (h + 1) * 128],
                                    "qfb", tb)
            if own:
                hcnt = {}
                for h in range(H):
                    par = h % 2
                    if h >= 2:
                        env.wait("g", "t", at=hcnt[h - 2])
                    else:
                        env.wait("g", "t")
                    env.dma("g",
                            sb["wtile"][:, par * CB * 128:
                                        (par + 1) * CB * 128].rearrange(
                                "p (c n) -> p c n", n=128),
                            t["gWgB"].ap().rearrange(
                                "(c p) d -> p c d",
                                p=128)[:, :, h * 128:(h + 1) * 128])
                    env.done("g", 16)
                    env.wait("t", "g")
                    env.wait("t", "s")
                    for cb2 in range(CB):
                        env.T.matmul(
                            ps[("pe", "pf")[par]][0:128, 0:QG],
                            sb["wtile"][:, par * CB * 128 + cb2 * 128:
                                        par * CB * 128 + (cb2 + 1) * 128],
                            sb["xnf"][:, cb2 * QG:(cb2 + 1) * QG],
                            start=(cb2 == 0), stop=(cb2 == CB - 1))
                    env.done("t")
                    hcnt[h] = env.count("t")
                    env.wait("s", "t")
                    env.S.activation(sb["sgf"][:, :],
                                     ps[("pe", "pf")[par]][0:128, 0:QG],
                                     AF.Silu)
                    env.done("s")
                    env.wait("y", "s")
                    env.dma("y",
                            t["SGsc"].ap()[h, (g0 - S0) // 128:
                                           (g0 - S0) // 128 + NTB].rearrange(
                                "b p q -> p b q"),
                            sb["sgf"][:, :].rearrange(
                                "p (b q) -> p b q", b=NTB))
                    env.done("y", 16)
                    env.wait("s", "y")
            if not own:
                env.wait("t", "v")
                for h in range(H):
                    for sc in range(NTB):
                        env.T.matmul(
                            ps["pe"][0:128, 0:128],
                            sb["ktb"][:, sc * D + h * 128:
                                      sc * D + (h + 1) * 128],
                            sb["vtb"][:, sc * D + h * 128:
                                      sc * D + (h + 1) * 128],
                            start=(sc == 0), stop=(sc == NTB - 1))
                    env.done("t")
                    env.wait("v", "t")
                    env.V.scalar_tensor_tensor(
                        sb["S_st"][:, h * 128:(h + 1) * 128],
                        sb["S_st"][:, h * 128:(h + 1) * 128],
                        float(gSG[h]), ps["pe"][0:128, 0:128],
                        ALU.mult, ALU.add)
                    env.done("v")
                    env.wait("t", "v")
            else:
                ob = ("pe", "pd")
                wb = ("pf", "pc")
                sbf_ = ("S_bf", "S_bf2")
                cnt_o = {}
                cnt_g = {}
                group_v0 = env.count("v")
                env.wait("t", "s")
                env.wait("t", "v")
                for h in range(H):
                    hb = h % 2
                    obk, wbk = ps[ob[hb]], ps[wb[hb]]
                    sbfn = sbf_[hb]
                    if h >= 2:
                        env.wait("t", "s", at=cnt_g[h - 2][0])
                        env.wait("t", "v", at=cnt_g[h - 2][1])
                    env.wait("s", "v", at=group_v0)
                    cSTT = None
                    for ck in range(NTB):
                        env.S.activation(sb[sbfn][:, :],
                                         sb["S_st"][:, h * 128:(h + 1) * 128],
                                         AF.Identity)
                        env.done("s")
                        cS = env.count("s")
                        if cSTT is not None:
                            env.wait("t", "v", at=cSTT)
                        env.T.matmul(wbk[0:128, (ck % 2) * 384:
                                         (ck % 2) * 384 + 128],
                                     sb["kfb"][:, h * QG + ck * 128:
                                               h * QG + (ck + 1) * 128],
                                     sb["qfb"][:, h * QG + ck * 128:
                                               h * QG + (ck + 1) * 128],
                                     start=True, stop=True)
                        env.done("t")
                        cA = env.count("t")
                        env.wait("v", "t", at=cA)
                        eoff = (hb * 2 + ck % 2) * 256
                        env.V.tensor_tensor(
                            sb["evb"][:, eoff:eoff + 128],
                            wbk[0:128, (ck % 2) * 384:(ck % 2) * 384 + 128],
                            sb["dmatT"][:, h * CK:(h + 1) * CK], ALU.mult)
                        env.V.tensor_tensor(
                            sb["evb"][:, eoff + 128:eoff + 256],
                            sb["qfb"][:, h * QG + ck * 128:
                                      h * QG + (ck + 1) * 128],
                            sb["crossC"][:, h * CK:(h + 1) * CK], ALU.mult)
                        env.done("v")
                        cM = env.count("v")
                        env.wait("t", "v", at=cM)
                        env.wait("t", "s", at=cS)
                        env.T.matmul(obk[0:128, ck * 128:(ck + 1) * 128],
                                     sb["vtb"][:, ck * D + h * 128:
                                               ck * D + (h + 1) * 128],
                                     sb["evb"][:, eoff:eoff + 128],
                                     start=True, stop=False)
                        env.T.matmul(obk[0:128, ck * 128:(ck + 1) * 128],
                                     sb[sbfn][:, :],
                                     sb["evb"][:, eoff + 128:eoff + 256],
                                     start=False, stop=True)
                        env.T.matmul(wbk[0:128, 128:256],
                                     sb["ktb"][:, ck * D + h * 128:
                                               ck * D + (h + 1) * 128],
                                     sb["vtb"][:, ck * D + h * 128:
                                               ck * D + (h + 1) * 128],
                                     start=True, stop=True)
                        env.done("t")
                        cU = env.count("t")
                        env.wait("v", "t", at=cU)
                        env.V.scalar_tensor_tensor(
                            sb["S_st"][:, h * 128:(h + 1) * 128],
                            sb["S_st"][:, h * 128:(h + 1) * 128],
                            float(gCK[h]), wbk[0:128, 128:256],
                            ALU.mult, ALU.add)
                        env.done("v")
                        cSTT = env.count("v")
                        if ck + 1 < NTB:
                            env.wait("s", "v", at=cSTT)
                    cnt_o[h] = env.count("t")
                    env.wait("s", "t", at=cnt_o[h])
                    env.S.activation(sb["gsq"][:, 0:QG],
                                     obk[0:128, 0:QG], AF.Square)
                    env.done("s")
                    c1 = env.count("s")
                    env.wait("t", "s", at=c1)
                    env.wait("t", "v", at=cSTT)
                    env.T.matmul(wbk[0:1, 0:QG], sb["onesbf"][:, 0:1],
                                 sb["gsq"][:, 0:QG], start=True, stop=True)
                    env.done("t")
                    c2 = env.count("t")
                    env.wait("s", "t", at=c2)
                    env.S.activation(sb["srow"][0:1, 0:QG],
                                     wbk[0:1, 0:QG], AF.Ln,
                                     scale=1.0 / DH, bias=cfg.EPS_GN)
                    env.done("s")
                    env.wait("s", "s")
                    env.S.activation(sb["srow"][0:1, 0:QG],
                                     sb["srow"][0:1, 0:QG], AF.Exp,
                                     scale=-0.5)
                    env.done("s")
                    c3 = env.count("s")
                    env.wait("t", "s", at=c3)
                    env.T.matmul(wbk[0:128, 0:QG], sb["onesfr"][0:1, :],
                                 sb["srow"][0:1, 0:QG], start=True, stop=True)
                    env.done("t")
                    c4 = env.count("t")
                    env.wait("s", "t", at=c4)
                    if h > 0:
                        env.wait("s", "v", at=cnt_g[h - 1][1])
                    env.S.activation(sb["evac"][:, 0:QG], wbk[0:128, 0:QG],
                                     AF.Identity, scale=sb["gnw"][:, 0:1])
                    env.done("s")
                    c5 = env.count("s")
                    if h > 0:
                        env.wait("g", "v", at=cnt_g[h - 1][1])
                    env.wait("g", "y")
                    env.dma("g", 
                        sb["sgf"][:, :].rearrange("p (b q) -> p b q", b=NTB),
                        t["SGsc"].ap()[h, (g0 - S0) // 128:
                                       (g0 - S0) // 128 + NTB].rearrange(
                            "b p q -> p b q"))
                    env.done("g", 16)
                    env.wait("v", "s", at=c5)
                    env.wait("v", "g")
                    env.V.tensor_tensor(sb["sq32"][:, 0:QG],
                                        obk[0:128, 0:QG],
                                        sb["evac"][:, 0:QG], ALU.mult)
                    env.done("v")
                    env.wait("v", "v")
                    gtoff = 1024 + hb * QG
                    env.V.tensor_tensor(sb["evb"][:, gtoff:gtoff + QG],
                                        sb["sq32"][:, 0:QG],
                                        sb["sgf"][:, :], ALU.mult)
                    env.done("v")
                    c6 = env.count("v")
                    env.wait("y", "v", at=c6)
                    env.dma("y",
                            t["GTsc"].ap()[h, (g0 - S0) // 128:
                                           (g0 - S0) // 128 + NTB].rearrange(
                                "b p q -> p b q"),
                            sb["evb"][:, gtoff:gtoff + QG].rearrange(
                                "p (b q) -> p b q", b=NTB))
                    env.done("y", 16)
                    cnt_g[h] = (c5, c6)
        mk.sb["xf"] = xbufs[0]
        mk.sb["xnf"] = xbufs[0]

    if upto >= 4:
        emit_phase(mk, "ph4", ph_gla, locals_={
            "xf": ((128, CB * QG), BF16),
            "xf2": ((128, CB * QG), BF16),
            "kraw": ((128, NTB * D), BF16),
            "ktb": ((128, NTB * D), BF16),
            "vtb": ((128, NTB * D), BF16),
            "qfb": ((128, H * QG), BF16),
            "kfb": ((128, H * QG), BF16),
            "evb": ((128, max(D, 2048)), BF16),
            "wrhs": ((128, 2 * min(8, CB) * DG), BF16),
            "sgf": ((128, QG), BF16),
            "gsq": ((128, QG), BF16),
            "tta": ((128, RW), F32), "ttb": ((128, RW), F32),
            "rpa": ((128, RW), F32), "rpb": ((128, RW), F32),
            "S_st": ((128, H * 128), F32),
            "S_bf": ((128, 128), BF16),
            "S_bf2": ((128, 128), BF16),
            "kdec128": ((128, D), F32),
            "crossC": ((128, H * CK), F32),
            "dmatT": ((CK, H * CK), F32),
            "rbc": ((128, QG), F32), "mbc": ((128, QG), F32)},
            aliases={"xnf": "xf", "sq32": "rpa", "evac": "rpb",
                     "wtile": "wrhs"})

    def ph_kvm(env):
        c = env.core
        E, S0 = end(c), start(c)
        env.G.memset(sb["stg"][:, :], 0.0)
        env.done("g")
        env.wait("g", "g")
        for h0 in range(H):
            env.G.memset(sb["stg"][:, h0 * 48:h0 * 48 + 1], 1.0)
        env.done("g")
        xbufs = (mk.sb["xf"], mk.sb["xf2"])
        x1bufs = (mk.sb["x1f"], mk.sb["x1f2"])

        def bind(gi):
            mk.sb["xf"] = xbufs[gi % 2]
            mk.sb["xnf"] = xbufs[gi % 2]
            mk.sb["x1f"] = x1bufs[gi % 2]

        def prep(gi, g0):
            bind(gi)
            load_xf_group(env, g0)
            build_bcasts(env, g0, rr=True)
            build_xn(env, with_x1=True)

        groups = list(range(0, E, QG))
        prep(0, 0)
        for gi, g0 in enumerate(groups):
            own = g0 >= S0
            nb0 = g0 // 128
            bind(gi)
            env.wait("t", "v")
            wlist = [("fWkTB", "Ksc", True)]
            if own:
                wlist.append(("fWqTB", "Qsc", False))
            for wname, scname, is_k in wlist:
                hcnt = {}
                for h in range(H):
                    par = h % 2
                    if h >= 2:
                        env.wait("g", "t", at=hcnt[h - 2])
                    else:
                        env.wait("g", "t")
                    env.dma("g",
                            sb["wtile"][:, par * CB * 128:
                                        (par + 1) * CB * 128].rearrange(
                                "p (c n) -> p c n", n=128),
                            t[wname].ap().rearrange(
                                "(c p) d -> p c d",
                                p=128)[:, :, h * 128:(h + 1) * 128])
                    env.done("g", 16)
                    env.wait("t", "g")
                    env.wait("t", "v")
                    env.wait("t", "s")
                    for cb2 in range(CB):
                        env.T.matmul(
                            ps["pa"][0:128, 0:QG],
                            sb["wtile"][:, par * CB * 128 + cb2 * 128:
                                        par * CB * 128 + (cb2 + 1) * 128],
                            sb["x1f"][:, cb2 * QG:(cb2 + 1) * QG],
                            start=(cb2 == 0), stop=(cb2 == CB - 1))
                    env.done("t")
                    hcnt[h] = env.count("t")
                    env.wait("v", "t")
                    env.V.tensor_copy(sb["evb"][:, 0:QG],
                                      ps["pa"][0:128, 0:QG])
                    env.done("v")
                    env.wait("y", "v")
                    base = nb0 if is_k else (g0 - S0) // 128
                    env.dma("y",
                            t[scname].ap()[h, base:base + NTB].rearrange(
                                "b p q -> p b q"),
                            sb["evb"][:, 0:QG].rearrange(
                                "p (b q) -> p b q", b=NTB))
                    env.done("y", 16)
                    if True:
                        env.wait("s", "v")
                        env.S.activation(sb["nsq"][:, 0:QG],
                                         sb["evb"][:, 0:QG], AF.Square)
                        env.done("s")
                        env.wait("t", "s")
                        env.T.matmul(ps["pb"][0:1, 0:QG],
                                     sb["onesbf"][:, 0:1],
                                     sb["nsq"][:, 0:QG],
                                     start=True, stop=True)
                        env.done("t")
                        env.wait("v", "t")
                        if is_k:
                            env.V.reduce_max(sb["srow"][0:1, 0:1],
                                             ps["pb"][0:1, 0:QG],
                                             axis=mybir.AxisListType.X)
                            env.done("v")
                            env.wait("v", "v")
                            env.V.tensor_max(sb["knorm"][0:1, h:h + 1],
                                             sb["knorm"][0:1, h:h + 1],
                                             sb["srow"][0:1, 0:1])
                            env.done("v")
                        else:
                            qi = (g0 - S0) // QG
                            env.V.reduce_max(
                                sb["qnorm"][0:1, qi * 16 + h:qi * 16 + h + 1],
                                ps["pb"][0:1, 0:QG], axis=mybir.AxisListType.X)
                            env.done("v")
                        env.wait("t", "v")
                    env.wait("v", "y")
            if gi + 1 < len(groups):
                prep(gi + 1, groups[gi + 1])
                bind(gi)
            for tb in range(NTB):
                env.wait("t", "v")
                for cb2 in range(CB):
                    env.T.matmul(
                        ps["pc"][0:128, 0:H * R],
                        sb["x1f"][:, cb2 * QG + tb * 128:
                                  cb2 * QG + (tb + 1) * 128],
                        sb["WVMr"][:, cb2 * H * R:(cb2 + 1) * H * R],
                        start=(cb2 == 0), stop=(cb2 == CB - 1))
                env.done("t")
                env.wait("v", "t")
                dstap = bass.AP(sb["stg"], 32,
                                [[sb["stg"].shape[1], 128], [48, H], [1, R]])
                env.V.tensor_copy(dstap, ps["pc"][0:128, 0:H * R])
                env.done("v")
                env.wait("y", "v")
                env.dma("y", t["VMsc"].ap()[nb0 + tb], sb["stg"][:, :])
                env.done("y", 16)
                env.wait("t", "y")
                env.wait("v", "y")
        mk.sb["xf"] = xbufs[0]
        mk.sb["xnf"] = xbufs[0]
        mk.sb["x1f"] = x1bufs[0]

    if upto >= 5:
        emit_phase(mk, "ph5", ph_kvm, locals_={
            "xf": ((128, CB * QG), BF16),
            "xf2": ((128, CB * QG), BF16),
            "x1f": ((128, CB * QG), BF16),
            "x1f2": ((128, CB * QG), BF16),
            "wtile": ((128, 2 * CB * 128), BF16),
            "evb": ((128, QG), BF16),
            "stg": ((128, H * 48), BF16),
            "nsq": ((128, QG), BF16),
            "rbc": ((128, QG), F32), "mbc": ((128, QG), F32),
            "rrbc": ((128, QG), F32)},
            aliases={"xnf": "xf"})

    def ph_attn(env):
        c = env.core
        S0 = start(c)
        NQ = n_own(c) // QG
        env.dma("g", sb["masks"][:, :], t["masks"].ap())
        env.done("g", 16)
        for h in range(H):
            for qi in range(NQ):
                env.wait("g", "t")
                env.dma("g",
                        sb["Qtile"][:, :].rearrange("p (b q) -> p b q",
                                                    b=NTB),
                        t["Qsc"].ap()[h, qi * NTB:(qi + 1) * NTB].rearrange(
                            "b p q -> p b q"))
                env.done("g", 16)
                env.wait("v", "g")
                env.V.tensor_tensor(
                    sb["srow"][0:1, 0:1],
                    sb["qnorm"][0:1, qi * 16 + h:qi * 16 + h + 1],
                    sb["knorm"][0:1, h:h + 1], ALU.mult)
                env.done("v")
                env.wait("s", "v")
                env.S.activation(sb["srow"][0:1, 1:2], sb["srow"][0:1, 0:1],
                                 AF.Ln)
                env.done("s")
                env.wait("s", "s")
                env.S.activation(sb["srow"][0:1, 1:2], sb["srow"][0:1, 1:2],
                                 AF.Exp, scale=0.5)
                env.done("s")
                env.wait("v", "s")
                env.V.tensor_scalar_mul(sb["srow"][0:1, 2:3],
                                        sb["srow"][0:1, 1:2], -ISQ)
                env.done("v")
                env.wait("t", "v")
                env.T.matmul(ps["pd"][0:128, 0:1], sb["onesfr"][0:1, :],
                             sb["srow"][0:1, 2:3], start=True, stop=True)
                env.done("t")
                env.wait("v", "t")
                env.V.tensor_copy(sb["bias"][:, 0:1], ps["pd"][0:128, 0:1])
                env.done("v")
                nsb = (S0 + (qi + 1) * QG) // 128
                KB4 = 4
                t_scores = {}
                s_exp = {}
                ksrc = t["Ksc"].ap()[h]
                for si in range(nsb):
                    par = si % 2
                    bpar = (si // KB4) % 2
                    if si % KB4 == 0:
                        nb_ = min(KB4, nsb - si)
                        if si >= 2 * KB4:
                            env.wait("g", "t",
                                     at=t_scores[si - KB4] +
                                     (2 if si - KB4 > 0 else 1))
                        env.dma("g",
                                sb["Ktile"][:, bpar * KB4 * 128:
                                            bpar * KB4 * 128 + nb_ * 128
                                            ].rearrange(
                                    "p (b n) -> p b n", n=128),
                                ksrc[si:si + nb_].rearrange("b p n -> p b n"))
                        env.done("g", 16)
                        env.dma("g",
                                sb["VMtile"][:, bpar * KB4 * 48:
                                             bpar * KB4 * 48 + nb_ * 48
                                             ].rearrange(
                                    "p (b n) -> p b n", n=48),
                                t["VMsc"].ap()[si:si + nb_, :,
                                               h * 48:(h + 1) * 48
                                               ].rearrange("b p n -> p b n"))
                        env.done("g", 16)
                        env.wait("t", "g")
                    if si == 0:
                        env.wait("t", "v")
                    pbank = ps["pa"] if par == 0 else ps["pb"]
                    if si >= 2:
                        env.wait("t", "s", at=s_exp[si - 2])
                    koff = (bpar * KB4 + si % KB4) * 128
                    t_scores[si] = env.count("t")
                    env.T.matmul(pbank[0:128, 0:QG],
                                 sb["Ktile"][:, koff:koff + 128],
                                 sb["Qtile"][:, :], start=True, stop=True)
                    env.done("t")
                    if si > 0:
                        env.wait("t", "s", at=s_exp[si - 1])
                        vmo = (((si - 1) // KB4) % 2) * KB4 * 48 + \
                            ((si - 1) % KB4) * 48
                        env.T.matmul(
                            ps["pc"][0:48, 0:QG],
                            sb["VMtile"][:, vmo:vmo + 48],
                            sb["ptile"][:, ((si - 1) % 2) * QG:
                                        ((si - 1) % 2 + 1) * QG],
                            start=(si - 1 == 0), stop=False)
                        env.done("t")
                    dspos = si - (S0 + qi * QG) // 128
                    if dspos >= 0:
                        env.wait("v", "t", at=t_scores[si] + 1)
                        env.V.tensor_tensor(
                            pbank[0:128, 0:QG], pbank[0:128, 0:QG],
                            sb["masks"][:, dspos * QG:(dspos + 1) * QG],
                            ALU.add)
                        env.done("v")
                        env.wait("s", "v")
                    else:
                        env.wait("s", "t", at=t_scores[si] + 1)
                    env.S.activation(sb["ptile"][:, par * QG:(par + 1) * QG],
                                     pbank[0:128, 0:QG], AF.Exp, scale=ISQ,
                                     bias=sb["bias"][:, 0:1])
                    env.done("s")
                    s_exp[si] = env.count("s")
                si = nsb - 1
                env.wait("t", "s", at=s_exp[si])
                vmo = ((si // KB4) % 2) * KB4 * 48 + (si % KB4) * 48
                env.T.matmul(ps["pc"][0:48, 0:QG],
                             sb["VMtile"][:, vmo:vmo + 48],
                             sb["ptile"][:, (si % 2) * QG:
                                         (si % 2 + 1) * QG],
                             start=(si == 0), stop=True)
                env.done("t")
                env.wait("v", "t")
                env.V.reciprocal(sb["srow"][0:1, 0:QG], ps["pc"][0:1, 0:QG])
                env.done("v")
                env.wait("s", "t")
                env.wait("s", "v")
                env.S.activation(sb["m16"][:, :], ps["pc"][32:48, 0:QG],
                                 AF.Identity)
                env.done("s")
                env.wait("t", "v")
                env.wait("t", "s")
                env.T.matmul(ps["pd"][0:16, 0:QG], sb["onesfr"][0:1, 0:16],
                             sb["srow"][0:1, 0:QG], start=True, stop=True)
                env.done("t")
                env.wait("v", "s")
                env.wait("v", "t")
                env.V.tensor_tensor(sb["m16"][:, :], sb["m16"][:, :],
                                    ps["pd"][0:16, 0:QG], ALU.mult)
                env.done("v")
                env.wait("v", "v")
                env.V.tensor_add(sb["modacc"][0:R, qi * QG:(qi + 1) * QG],
                                 sb["modacc"][0:R, qi * QG:(qi + 1) * QG],
                                 sb["m16"][:, :])
                env.done("v")
                env.wait("t", "v")
    wvm_stack.close()
    mk.sb.pop("WVMr", None)
    if upto >= 6:
        emit_phase(mk, "ph6", ph_attn, locals_={
            "Ktile": ((128, 8 * 128), BF16),
            "VMtile": ((128, 8 * 48), BF16),
            "Qtile": ((128, QG), BF16),
            "ptile": ((128, 2 * QG), BF16),
            "m16": ((16, QG), F32),
            "masks": ((128, NMV * QG), F32)})

    def ph_mlp_final(env):
        c = env.core
        n = n_own(c)
        env.wait("g", "b")
        S0 = start(c)
        nhalf = (n + MLPH - 1) // MLPH
        for hv in range(nhalf):
            t0 = hv * MLPH
            TL = min(MLPH, n - t0)
            for gg in range(0, TL, QG):
                g0 = S0 + t0 + gg
                load_xf_group(env, g0)
                build_bcasts(env, g0, rr=True)
                build_xn(env)
                env.wait("v", "v")
                for cb in range(CB):
                    sl = slice(cb * QG, (cb + 1) * QG)
                    env.V.scalar_tensor_tensor(
                        sb["hf"][:, cb * MLPH + gg:cb * MLPH + gg + QG],
                        sb["xnf"][:, sl], sb["ln2f"][:, cb:cb + 1],
                        sb["rrbc"][:, :], ALU.mult, ALU.mult)
                env.done("v")
                env.wait("g", "v")
                env.wait("t", "v")
            for gg in range(0, TL, 512):
                w = min(512, TL - gg)
                env.wait("t", "s")
                for cb in range(CB):
                    env.T.matmul(ps["pd"][0:R, 0:w],
                                 sb["A_bf"][:, cb * R:(cb + 1) * R],
                                 sb["hf"][:, cb * MLPH + gg:
                                           cb * MLPH + gg + w],
                                 start=(cb == 0), stop=(cb == CB - 1))
                env.done("t")
                env.wait("s", "t")
                env.S.activation(sb["hAT"][0:R, gg:gg + w],
                                 ps["pd"][0:R, 0:w], AF.Identity,
                                 scale=cfg.MOD_SCALE)
                env.done("s")
            env.wait("s", "s")
            env.S.activation(sb["mhT"][0:R, 0:TL],
                             sb["modacc"][0:R, t0:t0 + TL], AF.Tanh)
            env.done("s")
            env.wait("v", "s")
            env.V.tensor_tensor(sb["mhT"][0:R, 0:TL], sb["mhT"][0:R, 0:TL],
                                sb["hAT"][0:R, 0:TL], ALU.mult)
            env.done("v")
            env.wait("t", "v")
            def ld_inup(fi2, at_t):
                if at_t is not None:
                    env.wait("g", "t", at=at_t)
                else:
                    env.wait("g", "t")
                wz = (fi2 % 2) * 2 * CB * 128
                env.dma("g",
                        sb["wtile"][:, wz:wz + CB * 128].rearrange(
                            "p (c n) -> p c n", n=128),
                        t["WinTB"].ap().rearrange(
                            "(c p) d -> p c d",
                            p=128)[:, :, fi2 * 128:(fi2 + 1) * 128])
                env.done("g", 16)
                env.dma("g", sb["btile"][0:R, (fi2 % 2) * 128:
                                         (fi2 % 2) * 128 + 128],
                        t["Bmat"].ap()[:, fi2 * 128:(fi2 + 1) * 128])
                env.done("g", 16)
                env.dma("g",
                        sb["wtile"][:, wz + CB * 128:
                                    wz + 2 * CB * 128].rearrange(
                            "p (c n) -> p c n", n=128),
                        t["WupTB"].ap().rearrange(
                            "(c p) d -> p c d",
                            p=128)[:, :, fi2 * 128:(fi2 + 1) * 128])
                env.done("g", 16)

            fi_t = {}
            t_in = {}
            t_up = {}
            s_cnt = {}
            v_cnt = {}
            jj = 0
            ld_inup(0, None)
            for fg in range(FB // 8):
                for fb in range(8):
                    fi = fg * 8 + fb
                    slot = fi % 2
                    w0 = slot * 2 * CB * 128
                    env.wait("t", "g")
                    if fb == 0:
                        env.wait("t", "v")
                        env.wait("t", "s")
                    for gg in range(0, TL, 512):
                        w = min(512, TL - gg)
                        par = jj % 2
                        bin_ = ps[("pa", "pc")[par]]
                        bup = ps[("pb", "pd")[par]]
                        fg0 = (fb == 0 and gg == 0)
                        if jj >= 2 and not fg0:
                            env.wait("t", "s", at=s_cnt[jj - 2])
                        for cb in range(CB):
                            env.T.matmul(
                                bin_[0:128, 0:w],
                                sb["wtile"][:, w0 + cb * 128:
                                            w0 + (cb + 1) * 128],
                                sb["hf"][:, cb * MLPH + gg:
                                         cb * MLPH + gg + w],
                                start=(cb == 0), stop=False)
                        env.T.matmul(bin_[0:128, 0:w],
                                     sb["btile"][0:R, slot * 128:
                                                 slot * 128 + 128],
                                     sb["mhT"][0:R, gg:gg + w],
                                     start=False, stop=True)
                        env.done("t")
                        t_in[jj] = env.count("t")
                        if jj >= 2 and not fg0:
                            env.wait("t", "v", at=v_cnt[jj - 2])
                        for cb in range(CB):
                            env.T.matmul(
                                bup[0:128, 0:w],
                                sb["wtile"][:, w0 + CB * 128 + cb * 128:
                                            w0 + CB * 128 + (cb + 1) * 128],
                                sb["hf"][:, cb * MLPH + gg:
                                         cb * MLPH + gg + w],
                                start=(cb == 0), stop=(cb == CB - 1))
                        env.done("t")
                        t_up[jj] = env.count("t")
                        env.wait("s", "t", at=t_in[jj])
                        if jj >= 2:
                            env.wait("s", "v", at=v_cnt[jj - 2])
                        env.S.activation(sb["sgf"][:, par * QG:par * QG + w],
                                         bin_[0:128, 0:w], AF.Silu)
                        env.done("s")
                        s_cnt[jj] = env.count("s")
                        env.wait("v", "t", at=t_up[jj])
                        env.wait("v", "s", at=s_cnt[jj])
                        env.V.tensor_tensor(
                            sb["gustg"][:, fb * MLPH + gg:
                                        fb * MLPH + gg + w],
                            bup[0:128, 0:w],
                            sb["sgf"][:, par * QG:par * QG + w],
                            ALU.mult)
                        env.done("v")
                        v_cnt[jj] = env.count("v")
                        jj += 1
                    fi_t[fi] = env.count("t")
                    if fi + 1 < FB:
                        ld_inup(fi + 1, fi_t[fi - 1] if fi >= 1 else None)
                for hb8 in range(2):
                    env.wait("g", "t")
                    for fb in range(hb8 * 4, hb8 * 4 + 4):
                        fi2b = fg * 8 + fb
                        env.dma("g", sb["wdnb"][:, (fb % 4) * D:
                                                (fb % 4 + 1) * D],
                                t["WdnTB"].ap()[fi2b * 128:
                                                (fi2b + 1) * 128, :])
                        env.done("g", 16)
                    env.wait("t", "g")
                    env.wait("t", "v")
                    wd_t = {}
                    wd_v = {}
                    wd_i = 0
                    for tb in range(TL // 128):
                        for dg in range(D // DG):
                            bank = ("pc", "pd")[wd_i % 2]
                            if wd_i >= 2:
                                env.wait("t", "v", at=wd_v[wd_i - 2])
                            for fb in range(4):
                                env.T.matmul(
                                    ps[bank][0:128, 0:DG],
                                    sb["gustg"][:, (hb8 * 4 + fb) * MLPH +
                                                tb * 128:
                                                (hb8 * 4 + fb) * MLPH +
                                                (tb + 1) * 128],
                                    sb["wdnb"][:, fb * D + dg * DG:
                                               fb * D + (dg + 1) * DG],
                                    start=(fb == 0), stop=(fb == 3))
                            env.done("t")
                            wd_t[wd_i] = env.count("t")
                            env.wait("v", "t", at=wd_t[wd_i])
                            dst = sb["mlp"][:, tb * D + dg * DG:
                                            tb * D + (dg + 1) * DG]
                            if fg == 0 and hb8 == 0:
                                env.V.tensor_copy(dst, ps[bank][0:128, 0:DG])
                            else:
                                env.V.tensor_add(dst, dst,
                                                 ps[bank][0:128, 0:DG])
                            env.done("v")
                            wd_v[wd_i] = env.count("v")
                            wd_i += 1
            for dg in range(D // DG):
                env.wait("g", "t")
                env.dma("g",
                        sb["wdnb"][:, 0:CB * DG].rearrange(
                            "p (c n) -> p c n", n=DG),
                        t["gWoB"].ap().rearrange(
                            "(c p) d -> p c d",
                            p=128)[:, :, dg * DG:(dg + 1) * DG])
                env.done("g", 16)
                for tb in range(TL // 128):
                    tg = t0 // 128 + tb
                    env.wait("g", "t")
                    env.wait("g", "v")
                    env.wait("g", "s")
                    env.dma("g",
                            sb["gtile"][:, 0:H * 128].rearrange(
                                "p (c n) -> p c n", n=128),
                            t["GTsc"].ap()[:, tg].rearrange("c p n -> p c n"))
                    env.done("g", 16)
                    env.dma("g", 
                        sb["xtile"][:, :],
                        t["x"].ap()[S0 + t0 + tb * 128:
                                    S0 + t0 + (tb + 1) * 128,
                                    dg * DG:(dg + 1) * DG])
                    env.done("g", 16)
                    env.wait("t", "g")
                    env.wait("t", "v")
                    for h in range(H):
                        env.T.matmul(ps["pa"][0:128, 0:DG],
                                     sb["gtile"][:, h * 128:(h + 1) * 128],
                                     sb["wdnb"][:, h * DG:(h + 1) * DG],
                                     start=(h == 0), stop=(h == H - 1))
                    env.done("t")
                    env.wait("s", "g")
                    env.wait("s", "v")
                    env.wait("s", "y")
                    env.S.activation(sb["ftile"][:, :], sb["xtile"][:, :],
                                     AF.Identity,
                                     scale=sb["r_col"][:, tg:tg + 1],
                                     bias=sb["nmh_col"][:, tg:tg + 1])
                    env.done("s")
                    env.wait("v", "s")
                    env.V.tensor_add(sb["ftile"][:, :], sb["ftile"][:, :],
                                     sb["mlp"][:, tb * D + dg * DG:
                                               tb * D + (dg + 1) * DG])
                    env.done("v")
                    env.wait("s", "v")
                    env.S.activation(sb["ftile"][:, :], sb["ftile"][:, :],
                                     AF.Silu)
                    env.done("s")
                    env.wait("v", "s")
                    env.wait("v", "t")
                    env.V.tensor_tensor(sb["ftile"][:, :],
                                        ps["pa"][0:128, 0:DG],
                                        sb["ftile"][:, :], ALU.mult)
                    env.done("v")
                    env.wait("v", "v")
                    env.V.tensor_add(sb["ftile"][:, :], sb["ftile"][:, :],
                                     sb["xtile"][:, :])
                    env.done("v")
                    env.wait("y", "v")
                    env.dma("y", 
                        t["out"].ap()[t0 + tb * 128:t0 + (tb + 1) * 128,
                                      dg * DG:(dg + 1) * DG],
                        sb["ftile"][:, :])
                    env.done("y", 16)
                    env.wait("t", "y")
                    env.wait("v", "y")
                    env.wait("g", "s")
        env.wait("y", "y")
    if upto >= 7:
        emit_phase(mk, "ph7", ph_mlp_final, locals_={
            "xf": ((128, CB * QG), BF16),
            "hf": ((128, CB * MLPH), BF16),
            "gustg": ((128, 8 * MLPH), BF16),
            "wtile": ((128, 4 * CB * 128), BF16),
            "wdnb": ((128, 4 * D), BF16),
            "sgf": ((128, 2 * QG), BF16),
            "xtile": ((128, DG), F32), "ftile": ((128, DG), F32),
            "hAT": ((R, MLPH), BF16), "mhT": ((R, MLPH), BF16),
            "mlp": ((128, (MLPH // 128) * D), F32),
            "rbc": ((128, QG), F32), "mbc": ((128, QG), F32),
            "rrbc": ((128, QG), F32)},
            aliases={"xnf": "xf", "gtile": "wtile"})
    if upto < 7:
        with nc.Block("trailer") as b:
            @b.sync
            def _(y):
                strl = mk.semaphore("trailer_sem")
                y.dma_start(t["out"].ap()[0:1, 0:16],
                            sb["cvec"][0:1, 0:16]).then_inc(strl, 16)
                y.wait_ge(strl, 16)

    return mk


def make_inmaps(cfg, inputs):
    S, D = cfg.S, cfg.D
    tabs = host_tables(cfg)
    x = np.ascontiguousarray(np.asarray(inputs["L0_attn_x"],
                                        np.float32).reshape(S, D))
    cv = np.zeros((128, 16), np.float32)
    for i, v in enumerate([cfg.EPS_LN, cfg.EPS_RMS, cfg.EPS_GN, 1.0, -1.0,
                           1.0 / D, 1.0 / cfg.DH]):
        cv[:, i] = v
    f32 = lambda a: np.ascontiguousarray(np.asarray(a, np.float32))
    base = {
        "x": x,
        "xT": np.ascontiguousarray(x.T),
        "gWq": f32(inputs["L0_attn_Wq"]),
        "gWk": f32(inputs["L0_attn_Wk"]),
        "gWv": f32(inputs["L0_attn_Wv"]),
        "gWg": f32(inputs["L0_attn_Wg"]),
        "gWo": f32(inputs["L0_attn_Wo"]),
        "gnw": f32(inputs["L0_attn_g_norm_w"]).reshape(128, 1),
        "fWqT": np.ascontiguousarray(f32(inputs["L0_ffn_Wq"]).T),
        "fWkT": np.ascontiguousarray(f32(inputs["L0_ffn_Wk"]).T),
        "fWv": f32(inputs["L0_ffn_Wv"]),
        "fWo": f32(inputs["L0_ffn_Wo"]),
        "Amat": f32(inputs["L0_ffn_A"]),
        "Bmat": f32(inputs["L0_ffn_B"]),
        "WinT": np.ascontiguousarray(f32(inputs["L0_ffn_W_in_base"]).T),
        "WupT": np.ascontiguousarray(f32(inputs["L0_ffn_W_up_base"]).T),
        "WdnT": np.ascontiguousarray(f32(inputs["L0_ffn_W_down"]).T),
        "ln1f": np.ascontiguousarray(
            f32(inputs["L0_ffn_ln1_w"]).reshape(cfg.CB, 128).T),
        "ln2f": np.ascontiguousarray(
            f32(inputs["L0_ffn_ln2_w"]).reshape(cfg.CB, 128).T),
        "cvec": cv,
        "ropeC": tabs["ropeC"], "ropeS": tabs["ropeS"],
        "ropeCd": tabs["ropeCd"], "ropeSd": tabs["ropeSd"],
        "kdec128": tabs["kdec128"], "crossC": tabs["crossC"],
        "dmatT": tabs["dmatT"], "masks": tabs["masks"],
        "ident": tabs["ident"],
    }
    in_maps = []
    for c in range(cfg.n_cores):
        m = dict(base)
        m["cid"] = np.array([[c]], np.int32)
        in_maps.append(m)
    return in_maps


def kernel(**inputs):
    cfg = Cfg(splits=[1024, 1024, 1024, 1024, 1024, 1024, 1024, 1024])
    mk = build(cfg)
    in_maps = make_inmaps(cfg, inputs)
    res = run_bass_kernel_spmd(mk.nc, in_maps,
                               core_ids=list(range(cfg.n_cores)))
    outs = [res.results[c]["out"][0:cfg.splits[c]]
            for c in range(cfg.n_cores)]
    full = np.concatenate(outs, axis=0).astype(np.float32)
    return full.reshape(1, cfg.S, cfg.D)


# version 14 (final)
# speedup vs baseline: 1.1086x; 1.1086x over previous
import math
import numpy as np

import concourse.bass as bass
import concourse.mybir as mybir
from concourse.bass_utils import run_bass_kernel_spmd

F32 = mybir.dt.float32
BF16 = mybir.dt.bfloat16
I32 = mybir.dt.int32
AF = mybir.ActivationFunctionType
ALU = mybir.AluOpType


class Cfg:
    def __init__(self, S=8192, D=2048, F=8192, H=16, n_cores=8, splits=None,
                 mlph=1024):
        self.S, self.D, self.F, self.H = S, D, F, H
        self.DH = 128
        self.R = 16
        self.n_cores = n_cores
        self.CB = D // 128
        self.FB = F // 128
        self.QG = 512 if S >= 4096 else 256
        self.SG = self.QG
        self.CK = 128
        assert self.H * self.DH == D and self.CB == self.H
        if splits is None:
            splits = [S // n_cores] * n_cores
        assert sum(splits) == S and all(sp % self.QG == 0 for sp in splits)
        self.splits = splits
        self.starts = [sum(splits[:i]) for i in range(n_cores)]
        self.NMAX = max(splits)
        self.MLPH = min(mlph, self.NMAX)
        self.EPS_LN = 1e-5
        self.EPS_RMS = 1e-6
        self.EPS_GN = 1e-5
        self.MOD_SCALE = 0.1


def gammas(H):
    return 1.0 - np.exp2(-5.0 - np.arange(H, dtype=np.float64))


def host_tables(cfg):
    S, D, H, DH, CK, SG, QG = (cfg.S, cfg.D, cfg.H, cfg.DH, cfg.CK, cfg.SG,
                               cfg.QG)
    g = gammas(H)
    t = {}
    inv = 1.0 / (10000.0 ** (np.arange(0, DH, 2, dtype=np.float64) / DH))
    ang = np.arange(S, dtype=np.float64)[:, None] * inv[None, :]
    cosv, sinv = np.cos(ang), np.sin(ang)
    C = np.empty((S, D), np.float64)
    Sg = np.empty((S, D), np.float64)
    for h in range(H):
        C[:, h * DH:h * DH + 64] = cosv
        C[:, h * DH + 64:h * DH + 128] = cosv
        Sg[:, h * DH:h * DH + 64] = -sinv
        Sg[:, h * DH + 64:h * DH + 128] = sinv
    t["ropeC"] = C.astype(np.float32)
    t["ropeS"] = Sg.astype(np.float32)
    dec = np.empty((S, D), np.float64)
    e = (SG - 1 - (np.arange(S) % SG)).astype(np.float64)
    for h in range(H):
        dec[:, h * DH:(h + 1) * DH] = (g[h] ** e)[:, None]
    t["ropeCd"] = (C * dec).astype(np.float32)
    t["ropeSd"] = (Sg * dec).astype(np.float32)
    kd = np.empty((128, D), np.float64)
    for h in range(H):
        kd[:, h * DH:(h + 1) * DH] = (
            g[h] ** (CK - 1 - np.arange(128, dtype=np.float64)))[:, None]
    t["kdec128"] = kd.astype(np.float32)
    cc = np.empty((128, H * CK), np.float64)
    for h in range(H):
        cc[:, h * CK:(h + 1) * CK] = (
            g[h] ** (np.arange(CK, dtype=np.float64) + 1.0))[None, :]
    t["crossC"] = cc.astype(np.float32)
    dm = np.zeros((CK, H * CK), np.float64)
    idx = np.arange(CK, dtype=np.float64)
    rel = idx[None, :] - idx[:, None]
    for h in range(H):
        dm[:, h * CK:(h + 1) * CK] = np.where(
            rel >= 0, g[h] ** np.maximum(rel, 0.0), 0.0)
    t["dmatT"] = dm.astype(np.float32)
    nmv = QG // 128
    mv = np.zeros((nmv, 128, QG), np.float32)
    for r in range(nmv):
        srow = r * 128 + np.arange(128)[:, None]
        mv[r] = np.where(srow <= np.arange(QG)[None, :], 0.0, -1e30)
    t["masks"] = np.ascontiguousarray(
        mv.transpose(1, 0, 2).reshape(128, nmv * QG))
    t["ident"] = np.eye(128, dtype=np.float32)
    t["gSG"] = g ** SG
    t["gCK"] = g ** CK
    return t




class _Dummy:
    def __getattr__(self, name):
        def _f(*a, **k):
            return None
        return _f


class _Rec:
    def __init__(self, eng, env):
        object.__setattr__(self, "_eng", eng)
        object.__setattr__(self, "_env", env)

    def __getattr__(self, name):
        m = getattr(self._eng, name)

        def _f(*a, **k):
            r = m(*a, **k)
            if r is not None and hasattr(r, "then_inc"):
                self._env.last = r
            return r
        return _f


class Env:
    def __init__(self, mk, active, core):
        self.mk = mk
        self.active = active
        self.core = core
        self.cnt = mk.counters[core]
        self.last = None
        nc = mk.nc
        self._real = {"t": nc.tensor, "v": nc.vector, "s": nc.scalar,
                      "g": nc.gpsimd, "y": nc.sync}
        for e in "tvsgy":
            setattr(self, e.upper(),
                    _Rec(self._real[e], self) if active == e else _Dummy())

    NROT = 4
    ENG = {"c": "g", "b": "g"}

    def dma(self, ename, dst, src):
        rk = "rot_" + ename
        j = self.cnt.get(rk, 0) % self.NROT
        self.cnt[rk] = self.cnt.get(rk, 0) + 1
        key = f"dma_{ename}{j}"
        prev = self.cnt.get(key, 0)
        el = self.ENG.get(ename, ename)
        eng = self._real[el]
        if self.active == el:
            if prev > 0:
                eng.wait_ge(self.mk.sem[key], prev)
            self.last = eng.dma_start(dst, src)
        self._pending_dma_key = key

    def done(self, ename, n=1):
        if ename in ("g", "y", "c", "b") and n == 16:
            key = self._pending_dma_key
        else:
            key = "done_" + ename
        self.cnt[key] = self.cnt.get(key, 0) + n
        if self.active == self.ENG.get(ename, ename):
            assert self.last is not None, f"done({ename}) with no inst"
            self.last.then_inc(self.mk.sem[key], n)
            self.last = None

    def wait(self, consumer, producer, at=None):
        if at is not None:
            if self.active == consumer and at > 0:
                self._real[consumer].wait_ge(self.mk.sem["done_" + producer],
                                             at)
            return
        keys = ["done_" + producer]
        if producer in ("g", "y", "c", "b"):
            keys += [f"dma_{producer}{j}" for j in range(self.NROT)]
        wkey = "w_" + consumer
        seen = self.cnt.setdefault(wkey, {}) if False else None
        for key in keys:
            val = self.cnt.get(key, 0)
            lastk = f"lw_{consumer}_{key}"
            if self.cnt.get(lastk, -1) == val:
                continue
            self.cnt[lastk] = val
            if self.active == consumer and val > 0:
                self._real[consumer].wait_ge(self.mk.sem[key], val)

    def count(self, ename):
        return self.cnt.get("done_" + ename, 0)


class MK:
    def __init__(self, cfg):
        self.cfg = cfg
        self.nc = bass.Bass("TRN2", target_bir_lowering=False)
        self.t = {}
        self.sb = {}
        self.ps = {}
        self.sem = {}
        self.exits = []
        self.counters = [dict() for _ in range(cfg.n_cores)]
        self.cid_regs = {}

    def dram(self, name, shape, dtype, kind="ExternalInput"):
        self.t[name] = self.nc.dram_tensor(name, list(shape), dtype, kind=kind)

    def sbuf(self, name, shape, dtype):
        self.sb[name] = self.nc.alloc_sbuf_tensor("sb_" + name, list(shape), dtype)

    def psum(self, name, shape, dtype=F32):
        self.ps[name] = self.nc.alloc_psum_tensor(name, list(shape), dtype)

    def semaphore(self, name):
        cm = self.nc.semaphore(name)
        s = cm.__enter__()
        self.exits.append(cm)
        self.sem[name] = s
        return s


def emit_phase(mk, name, fn, shared=False, locals_=None, aliases=None):
    mk.phase_marks = getattr(mk, "phase_marks", [])
    mk.phase_marks.append((name, mk.nc.n_instructions()))
    """Emit one Block.  fn(env) is executed once per engine pass; counters
    are snapshot/restored so each pass sees identical counts, and the final
    state (all passes produce identical deltas) is kept.  locals_ maps
    sbuf-name -> (shape, dtype) allocated for this phase only; aliases maps
    name -> existing local name (same tensor)."""
    import contextlib
    nc = mk.nc
    ncores = mk.cfg.n_cores
    stack = contextlib.ExitStack()
    added = []
    for nm, (shape, dtype) in (locals_ or {}).items():
        h = stack.enter_context(
            nc.sbuf_tensor(f"L{name}_{nm}", list(shape), dtype))
        mk.sb[nm] = h
        added.append(nm)
    for nm, src_nm in (aliases or {}).items():
        mk.sb[nm] = mk.sb[src_nm]
        added.append(nm)
    with nc.Block(name) as b:
        entry = [dict(d) for d in mk.counters]
        final = [None] * ncores

        def runner(ename):
            def _run(_eng):
                for c in range(ncores if not shared else 1):
                    mk.counters[c].clear()
                    mk.counters[c].update(entry[c])
                if shared:
                    fn(Env(mk, ename, 0))
                else:
                    reg = mk.cid_regs[ename]
                    for c in range(ncores):
                        with _eng.If_eq(reg, c):
                            fn(Env(mk, ename, c))
                for c in range(ncores if not shared else 1):
                    if final[c] is None:
                        final[c] = dict(mk.counters[c])
                    else:
                        assert final[c] == mk.counters[c], \
                            f"engine passes disagree in {name} core {c}"
            return _run
        b.tensor(runner("t"))
        b.vector(runner("v"))
        b.scalar(runner("s"))
        b.gpsimd(runner("g"))
        b.sync(runner("y"))
        if shared:
            d0 = mk.counters[0]
            for c in range(1, ncores):
                dc = mk.counters[c]
                keys = set(d0) | set(entry[0])
                for k in keys:
                    delta = d0.get(k, 0) - entry[0].get(k, 0)
                    if delta:
                        dc[k] = dc.get(k, 0) + delta
    stack.close()
    for nm in added:
        mk.sb.pop(nm, None)


def build(cfg, upto=99):
    mk = MK(cfg)
    nc = mk.nc
    S, D, F, H, R = cfg.S, cfg.D, cfg.F, cfg.H, cfg.R
    CB, FB, QG, SG, CK, DH = cfg.CB, cfg.FB, cfg.QG, cfg.SG, cfg.CK, cfg.DH
    NMAX, MLPH = cfg.NMAX, cfg.MLPH
    NMV = QG // 128
    NTB = QG // 128
    NDG = D // 512 if D >= 512 else 1
    DG = min(512, D)
    tabs = host_tables(cfg)
    gSG, gCK = tabs["gSG"], tabs["gCK"]
    ISQ = 1.0 / math.sqrt(DH)

    mk.dram("x", (S, D), F32)
    mk.dram("xT", (D, S), F32)
    for n in ("gWq", "gWk", "gWv", "gWg", "gWo", "fWqT", "fWkT", "fWv",
              "fWo"):
        mk.dram(n, (D, D), F32)
    mk.dram("gnw", (128, 1), F32)
    mk.dram("Amat", (D, R), F32)
    mk.dram("Bmat", (R, F), F32)
    mk.dram("WinT", (D, F), F32)
    mk.dram("WupT", (D, F), F32)
    mk.dram("WdnT", (F, D), F32)
    mk.dram("ln1f", (128, CB), F32)
    mk.dram("ln2f", (128, CB), F32)
    mk.dram("cid", (1, 1), I32)
    mk.dram("cvec", (128, 16), F32)
    for n in ("ropeC", "ropeS", "ropeCd", "ropeSd"):
        mk.dram(n, (S, D), F32)
    mk.dram("kdec128", (128, D), F32)
    mk.dram("crossC", (128, H * CK), F32)
    mk.dram("dmatT", (CK, H * CK), F32)
    mk.dram("masks", (128, NMV * QG), F32)
    mk.dram("ident", (128, 128), F32)
    mk.dram("out", (NMAX, D), F32, kind="ExternalOutput")
    NB = S // 128
    NBO = NMAX // 128
    mk.dram("Ksc", (H, NB, 128, 128), BF16, kind="Internal")
    mk.dram("VMsc", (NB, 128, H * 48), BF16, kind="Internal")
    mk.dram("Qsc", (H, NBO, 128, 128), BF16, kind="Internal")
    mk.dram("GTsc", (H, NBO, 128, 128), BF16, kind="Internal")
    mk.dram("SGsc", (H, NBO, 128, 128), BF16, kind="Internal")
    for n in ("gWqB", "gWkB", "gWvB", "gWgB", "fWkTB", "fWqTB"):
        mk.dram(n, (D, D), BF16, kind="Internal")
    mk.dram("WinTB", (D, F), BF16, kind="Internal")
    mk.dram("WupTB", (D, F), BF16, kind="Internal")
    mk.dram("WdnTB", (F, D), BF16, kind="Internal")
    mk.dram("gWoB", (D, D), BF16, kind="Internal")
    mk.dram("rowsD", (3, S), F32, kind="Internal")

    sbf = mk.sbuf
    sbf("cid", (1, 1), I32)
    sbf("cvec", (128, 16), F32)
    sbf("ident", (128, 128), BF16)
    sbf("onesbf", (128, 16), BF16)
    sbf("onesf", (128, 1), F32)
    sbf("onesfr", (1, 128), F32)
    sbf("ln1f", (128, CB), F32)
    sbf("ln2f", (128, CB), F32)
    sbf("gnw", (128, 1), F32)
    sbf("r_col", (128, NBO), F32)
    sbf("nmh_col", (128, NBO), F32)
    sbf("A_bf", (128, CB * R), BF16)
    sbf("knorm", (1, 16), F32)
    sbf("qnorm", (1, 16 * 16), F32)
    sbf("srow", (1, QG), F32)
    sbf("rows3", (1, 3 * QG), F32)
    sbf("modacc", (R, NMAX), F32)
    sbf("btile", (16, 2 * 128), BF16)
    sbf("bias", (128, 1), F32)
    RW = min(D, 512)

    for n in ("pa", "pb", "pc", "pd", "pe", "pf"):
        mk.psum(n, (128, 512))
    mk.psum("pt", (128, 1024), BF16)
    ps = mk.ps

    for e in "tvsgy":
        mk.semaphore("done_" + e)
    for j in range(Env.NROT):
        mk.semaphore(f"dma_g{j}")
        mk.semaphore(f"dma_y{j}")
        mk.semaphore(f"dma_c{j}")
        mk.semaphore(f"dma_b{j}")

    CONSTS = [cfg.EPS_LN, cfg.EPS_RMS, cfg.EPS_GN, 1.0, -1.0,
              1.0 / D, 1.0 / DH]
    for i, v in enumerate(CONSTS):
        nc.const_aps.register(F32, float(v), mk.sb["cvec"][:, i:i + 1])

    sb, t = mk.sb, mk.t

    def n_own(c): return cfg.splits[c]
    def start(c): return cfg.starts[c]
    def end(c): return cfg.starts[c] + cfg.splits[c]

    with nc.Block("ph0") as b:
        @b.gpsimd
        def _(g):
            sld = mk.semaphore("ld0")
            sld2 = mk.semaphore("ld0b")
            n = [0]

            def ld(dst, src):
                g.dma_start(dst, src).then_inc(sld, 16)
                n[0] += 16
            ld(sb["cid"][:, :], t["cid"].ap())
            ld(sb["cvec"][:, :], t["cvec"].ap())
            ld(sb["ident"][:, :], t["ident"].ap())
            ld(sb["ln1f"][:, :], t["ln1f"].ap())
            ld(sb["ln2f"][:, :], t["ln2f"].ap())
            ld(sb["gnw"][:, :], t["gnw"].ap())
            g.wait_ge(sld, n[0])
            g.memset(sb["onesbf"][:, :], 1.0)
            g.memset(sb["onesf"][:, :], 1.0)
            g.memset(sb["onesfr"][:, :], 1.0)
            g.memset(sb["knorm"][:, :], 0.0)
            g.memset(sb["qnorm"][:, :], 0.0)
            g.memset(sb["modacc"][:, :], 0.0).then_inc(sld2, 1)
            g.wait_ge(sld2, 1)

    with nc.Block("ph0b") as b:
        def mkload(ename):
            def _f(eng):
                r = eng.alloc_register("cid_" + ename)
                eng.reg_load(r, sb["cid"][:1, :1])
                mk.cid_regs[ename] = r
            return _f
        b.tensor(mkload("t"))
        b.vector(mkload("v"))
        b.scalar(mkload("s"))
        b.gpsimd(mkload("g"))
        b.sync(mkload("y"))

    def ph_wcast(env):
        def cast(q, dst, src):
            key = f"dma_{q}0"
            env.cnt[key] = env.cnt.get(key, 0) + 16
            if env.active == "g":
                env._real["g"].dma_start(dst, src).then_inc(mk.sem[key], 16)
        for src, dst in (("gWq", "gWqB"), ("gWk", "gWkB"), ("gWv", "gWvB"),
                         ("gWg", "gWgB"), ("fWkT", "fWkTB"),
                         ("fWqT", "fWqTB")):
            cast("c", t[dst].ap(), t[src].ap())
    if upto >= 1:
        emit_phase(mk, "ph1", ph_wcast, shared=True)

    def ph_stats(env):
        c = env.core
        E = end(c)
        for g0 in range(0, E, QG):
            env.wait("g", "g")
            env.wait("g", "s")
            env.wait("g", "t")
            env.dma("g", sb["xf"][:, :].rearrange("p (c q) -> p c q", c=CB),
                    t["xT"].ap().rearrange("(c p) s -> p c s",
                                           p=128)[:, :, g0:g0 + QG])
            env.done("g", 16)
            env.wait("s", "g")
            env.wait("s", "t")
            for cb in range(CB):
                env.S.activation(sb["sqf"][:, cb * QG:(cb + 1) * QG],
                                 sb["xf"][:, cb * QG:(cb + 1) * QG],
                                 AF.Square)
            env.done("s")
            env.wait("t", "s")
            env.wait("t", "v")
            env.wait("t", "y")
            for cb in range(CB):
                sl = slice(cb * QG, (cb + 1) * QG)
                env.T.matmul(ps["pa"][0:1, 0:QG], sb["onesbf"][:, 0:1],
                             sb["xf"][:, sl], start=(cb == 0),
                             stop=(cb == CB - 1))
            for cb in range(CB):
                sl = slice(cb * QG, (cb + 1) * QG)
                env.T.matmul(ps["pb"][0:1, 0:QG], sb["onesbf"][:, 0:1],
                             sb["sqf"][:, sl], start=(cb == 0),
                             stop=(cb == CB - 1))
            env.done("t")
            env.wait("v", "t")
            env.wait("v", "y")
            env.V.tensor_scalar_mul(sb["rwM"][0:1, :], ps["pa"][0:1, 0:QG],
                                    1.0 / D)
            env.V.tensor_scalar_mul(sb["rwS"][0:1, :], ps["pb"][0:1, 0:QG],
                                    1.0 / D)
            env.done("v")
            env.wait("v", "v")
            env.V.scalar_tensor_tensor(sb["rwV"][0:1, :], sb["rwM"][0:1, :],
                                       -1.0, sb["rwM"][0:1, :],
                                       ALU.mult, ALU.mult)
            env.done("v")
            env.wait("v", "v")
            env.V.tensor_add(sb["rwV"][0:1, :], sb["rwV"][0:1, :],
                             sb["rwS"][0:1, :])
            env.done("v")
            env.wait("s", "v")
            env.S.activation(sb["rwR"][0:1, :], sb["rwV"][0:1, :], AF.Ln,
                             bias=cfg.EPS_LN)
            env.done("s")
            env.wait("s", "s")
            env.S.activation(sb["rwR"][0:1, :], sb["rwR"][0:1, :], AF.Exp,
                             scale=-0.5)
            env.done("s")
            env.wait("v", "s")
            env.V.tensor_tensor(sb["rwH"][0:1, :], sb["rwM"][0:1, :],
                                sb["rwR"][0:1, :], ALU.mult)
            env.V.tensor_tensor(sb["rwS"][0:1, :], sb["rwR"][0:1, :],
                                sb["rwR"][0:1, :], ALU.mult)
            env.done("v")
            env.wait("v", "v")
            env.V.tensor_tensor(sb["rwS"][0:1, :], sb["rwS"][0:1, :],
                                sb["rwV"][0:1, :], ALU.mult)
            env.done("v")
            env.wait("s", "v")
            env.S.activation(sb["rwS"][0:1, :], sb["rwS"][0:1, :], AF.Ln,
                             bias=cfg.EPS_RMS)
            env.done("s")
            env.wait("s", "s")
            env.S.activation(sb["rwS"][0:1, :], sb["rwS"][0:1, :], AF.Exp,
                             scale=-0.5)
            env.done("s")
            env.wait("y", "s")
            env.wait("y", "v")
            env.dma("y", t["rowsD"].ap()[0:1, g0:g0 + QG], sb["rwR"][0:1, :])
            env.done("y", 16)
            env.dma("y", t["rowsD"].ap()[1:2, g0:g0 + QG], sb["rwH"][0:1, :])
            env.done("y", 16)
            env.dma("y", t["rowsD"].ap()[2:3, g0:g0 + QG], sb["rwS"][0:1, :])
            env.done("y", 16)
            if g0 >= start(c):
                env.wait("t", "s")
                env.wait("t", "v")
                for tb in range(QG // 128):
                    rs2 = slice(tb * 128, (tb + 1) * 128)
                    env.T.matmul(ps["pc"][0:128, tb * 2:tb * 2 + 1],
                                 sb["rwR"][0:1, rs2], sb["onesf"][0:1, 0:1],
                                 start=True, stop=True)
                    env.T.matmul(ps["pc"][0:128, tb * 2 + 1:tb * 2 + 2],
                                 sb["rwH"][0:1, rs2], sb["onesf"][0:1, 0:1],
                                 start=True, stop=True)
                env.done("t")
                env.wait("v", "t")
                for tb in range(QG // 128):
                    b0 = (g0 - start(c)) // 128 + tb
                    env.V.tensor_copy(sb["r_col"][:, b0:b0 + 1],
                                      ps["pc"][0:128, tb * 2:tb * 2 + 1])
                    env.V.tensor_scalar_mul(
                        sb["nmh_col"][:, b0:b0 + 1],
                        ps["pc"][0:128, tb * 2 + 1:tb * 2 + 2], -1.0)
                env.done("v")
                env.wait("t", "v")
        env.wait("g", "c")
    def ph_wvm(env):
        for cb in range(CB):
            env.dma("g", sb["A_bf"][:, cb * R:(cb + 1) * R],
                            t["Amat"].ap()[cb * 128:(cb + 1) * 128, :])
            env.done("g", 16)
        dbt = {}
        for db in range(2 * CB):
            slot = db % 2
            w0 = slot * CB * 128
            if db >= 2:
                env.wait("g", "t", at=dbt[db - 2])
            wn, col = (("fWo", db) if db < CB else ("fWv", db - CB))
            env.dma("g",
                    sb["wtile"][:, w0:w0 + CB * 128].rearrange(
                        "p (c n) -> p c n", n=128),
                    t[wn].ap().rearrange(
                        "(c p) d -> p c d",
                        p=128)[:, :, col * 128:(col + 1) * 128])
            env.done("g", 16)
            env.wait("t", "g")
            env.wait("t", "v")
            if db < CB:
                for cb in range(CB):
                    env.T.matmul(ps["pa"][0:128, 0:R],
                                 sb["wtile"][:, w0 + cb * 128:
                                             w0 + (cb + 1) * 128],
                                 sb["A_bf"][:, cb * R:(cb + 1) * R],
                                 start=(cb == 0), stop=(cb == CB - 1))
                env.done("t")
                dbt[db] = env.count("t")
                env.wait("v", "t")
                env.V.tensor_copy(sb["Mh"][:, db * R:(db + 1) * R],
                                  ps["pa"][0:128, 0:R])
                env.done("v")
            else:
                cbx = db - CB
                for h in range(H):
                    env.T.matmul(ps["pa"][0:128, h * R:(h + 1) * R],
                                 sb["wtile"][:, w0 + h * 128:
                                             w0 + (h + 1) * 128],
                                 sb["Mh"][:, h * R:(h + 1) * R],
                                 start=True, stop=True)
                env.done("t")
                dbt[db] = env.count("t")
                env.wait("v", "t")
                env.V.tensor_copy(sb["WVMr"][:, cbx * H * R:
                                             (cbx + 1) * H * R],
                                  ps["pa"][0:128, 0:H * R])
                env.done("v")
        env.wait("t", "v")
    import contextlib
    wvm_stack = contextlib.ExitStack()
    mk.sb["WVMr"] = wvm_stack.enter_context(
        nc.sbuf_tensor("sb_WVMr", [128, CB * H * R], BF16))
    if upto >= 3:
        emit_phase(mk, "ph3", ph_wvm, shared=True,
                   locals_={"wtile": ((128, 2 * CB * 128), BF16),
                            "Mh": ((128, CB * R), BF16)})
    if upto >= 2:
        emit_phase(mk, "ph2", ph_stats, locals_={
            "xf": ((128, CB * QG), BF16),
            "sqf": ((128, CB * QG), BF16),
            "rwM": ((1, QG), F32), "rwV": ((1, QG), F32),
            "rwR": ((1, QG), F32), "rwH": ((1, QG), F32),
            "rwS": ((1, QG), F32)})

    def load_xf_group(env, g0):
        env.wait("g", "g")
        env.dma("g", sb["xf"][:, :].rearrange("p (c q) -> p c q", c=CB),
                t["xT"].ap().rearrange("(c p) s -> p c s",
                                       p=128)[:, :, g0:g0 + QG])
        env.done("g", 16)

    def build_bcasts(env, g0, rr=False):
        env.wait("g", "t")
        env.wait("g", "v")
        env.wait("g", "y")
        nload = 3 if rr else 2
        for j in range(nload):
            env.dma("g", sb["rows3"][0:1, j * QG:(j + 1) * QG],
                    t["rowsD"].ap()[j:j + 1, g0:g0 + QG])
            env.done("g", 16)
        names = [(0, "rbc"), (1, "mbc")]
        if rr:
            names.append((2, "rrbc"))
        env.wait("t", "g")
        for j, bcn in names:
            env.wait("t", "v")
            env.T.matmul(ps["pf"][0:128, 0:QG], sb["onesfr"][0:1, :],
                         sb["rows3"][0:1, j * QG:(j + 1) * QG],
                         start=True, stop=True)
            env.done("t")
            env.wait("v", "t")
            env.wait("v", "g")
            env.wait("v", "s")
            env.V.tensor_copy(sb[bcn][:, :], ps["pf"][0:128, 0:QG])
            env.done("v")
        env.wait("t", "v")

    def build_xn(env, with_x1=False):
        env.wait("g", "v")
        env.wait("g", "t")
        env.wait("g", "s")
        for cb in range(CB):
            sl = slice(cb * QG, (cb + 1) * QG)
            env.G.tensor_tensor(sb["xf"][:, sl], sb["xf"][:, sl],
                                sb["rbc"][:, :], ALU.mult)
        env.done("g")
        env.wait("v", "g")
        env.wait("v", "t")
        for cb in range(CB):
            sl = slice(cb * QG, (cb + 1) * QG)
            env.V.tensor_tensor(sb["xnf"][:, sl], sb["xf"][:, sl],
                                sb["mbc"][:, :], ALU.subtract)
        env.done("v")
        if with_x1:
            env.wait("v", "v")
            for cb in range(CB):
                sl = slice(cb * QG, (cb + 1) * QG)
                env.V.scalar_tensor_tensor(sb["x1f"][:, sl], sb["xnf"][:, sl],
                                           sb["ln1f"][:, cb:cb + 1],
                                           sb["rrbc"][:, :],
                                           ALU.mult, ALU.mult)
            env.done("v")

    def proj_token_major(env, wname, dstb, extra_wait=()):
        banks = ("pa", "pb", "pc", "pd")[:NTB]
        SLC = min(8, CB)
        NSL = CB // SLC
        wsrc = t[wname].ap().rearrange("(c p) d -> p c d", p=128)
        tcnt = {}
        slab_i = 0
        for dg in range(D // DG):
            for sl in range(NSL):
                par = slab_i % 2
                if slab_i >= 2:
                    env.wait("g", "t", at=tcnt[slab_i - 2])
                else:
                    env.wait("g", "t")
                env.dma("g",
                        sb["wrhs"][:, par * SLC * DG:
                                   (par + 1) * SLC * DG].rearrange(
                            "p (c d) -> p c d", d=DG),
                        wsrc[:, sl * SLC:(sl + 1) * SLC,
                             dg * DG:(dg + 1) * DG])
                env.done("g", 16)
                env.wait("t", "g")
                if slab_i == 0:
                    env.wait("t", "s")
                for c2 in range(SLC):
                    cb2 = sl * SLC + c2
                    for tb in range(NTB):
                        env.T.matmul(
                            ps[banks[tb]][0:128, 0:DG],
                            sb["xnf"][:, cb2 * QG + tb * 128:
                                      cb2 * QG + (tb + 1) * 128],
                            sb["wrhs"][:, (par * SLC + c2) * DG:
                                       (par * SLC + c2 + 1) * DG],
                            start=(cb2 == 0), stop=(cb2 == CB - 1))
                env.done("t")
                tcnt[slab_i] = env.count("t")
                slab_i += 1
            env.wait("s", "t")
            env.wait("s", "g")
            env.wait("s", "v")
            for tb in range(NTB):
                env.S.activation(
                    sb[dstb][:, tb * D + dg * DG:tb * D + (dg + 1) * DG],
                    ps[banks[tb]][0:128, 0:DG], AF.Identity)
            env.done("s")
            env.wait("t", "s")


    def rope_blk(env, srcb, tb, p0, cn, sn, dst_slot):
        for hw_ in range(D // RW):
            off = tb * D + hw_ * RW
            env.wait("g", "g")
            env.wait("g", "v")
            env.dma("g", sb["tta"][:, :],
                    t[cn].ap()[p0:p0 + 128, hw_ * RW:(hw_ + 1) * RW])
            env.done("g", 16)
            env.dma("g", sb["ttb"][:, :],
                    t[sn].ap()[p0:p0 + 128, hw_ * RW:(hw_ + 1) * RW])
            env.done("g", 16)
            env.wait("g", "s")
            env.wait("g", "g")
            env.G.tensor_tensor(sb["rpa"][:, :],
                                sb[srcb][:, off:off + RW],
                                sb["tta"][:, :],
                                ALU.mult)
            env.done("g")
            sap = sb[srcb][:, :]
            swap = bass.AP(sap.tensor, sap.offset + off + 64,
                           [[sap.tensor.shape[1], 128],
                            [128, RW // 128], [-64, 2], [1, 64]])
            env.wait("v", "s")
            env.wait("v", "g")
            env.V.tensor_tensor(sb["rpb"][:, :], swap, sb["ttb"][:, :],
                                ALU.mult)
            env.done("v")
            env.wait("v", "g")
            env.wait("v", "v")
            env.wait("v", "t")
            env.wait("v", "y")
            dst = (sb[dst_slot][:, hw_ * RW:(hw_ + 1) * RW]
                   if dst_slot == "evb"
                   else sb[dst_slot][:, off:off + RW])
            env.V.tensor_tensor(dst, sb["rpa"][:, :], sb["rpb"][:, :],
                                ALU.add)
            env.done("v")

    def transpose_heads(env, srcap_fn, dstb, tb):
        for h in range(H):
            env.T.transpose(ps["pt"][0:128, (h % 8) * 128:(h % 8) * 128 + 128],
                            srcap_fn(h), sb["ident"][:, :])
            if h % 8 == 7 or h == H - 1:
                env.done("t")
                env.wait("s", "t")
                lo = h - (h % 8)
                for hh in range(lo, h + 1):
                    env.S.activation(
                        sb[dstb][:, hh * QG + tb * 128:
                                 hh * QG + (tb + 1) * 128],
                        ps["pt"][0:128, (hh % 8) * 128:(hh % 8) * 128 + 128],
                        AF.Identity)
                env.done("s")
                env.wait("t", "s")

    def ph_gla(env):
        c = env.core
        E, S0 = end(c), start(c)
        env.G.memset(sb["S_st"][:, :], 0.0)
        env.done("g")
        env.dma("g", sb["kdec128"][:, :], t["kdec128"].ap())
        env.done("g", 16)
        env.dma("g", sb["crossC"][:, :], t["crossC"].ap())
        env.done("g", 16)
        env.dma("g", sb["dmatT"][0:CK, :], t["dmatT"].ap())
        env.done("g", 16)
        xbufs = (mk.sb["xf"], mk.sb["xf2"])

        def bind(gi):
            mk.sb["xf"] = xbufs[gi % 2]
            mk.sb["xnf"] = xbufs[gi % 2]

        def prep(gi, g0):
            bind(gi)
            load_xf_group(env, g0)
            build_bcasts(env, g0)
            build_xn(env)

        groups = list(range(0, E, SG))
        prep(0, 0)
        for gi, g0 in enumerate(groups):
            own = g0 >= S0
            bind(gi)
            env.wait("t", "v")
            proj_token_major(env, "gWkB", "kraw")
            proj_token_major(env, "gWvB", "vtb")
            if gi + 1 < len(groups):
                prep(gi + 1, groups[gi + 1])
                bind(gi)
            for tb in range(NTB):
                p0 = g0 + tb * 128
                cn, sn = ("ropeC", "ropeS") if own else ("ropeCd", "ropeSd")
                if own:
                    rope_blk(env, "kraw", tb, p0, cn, sn, "evb")
                    env.wait("v", "v")
                    env.V.tensor_tensor(sb["ktb"][:, tb * D:(tb + 1) * D],
                                        sb["evb"][:, 0:D], sb["kdec128"][:, :],
                                        ALU.mult)
                    env.done("v")
                    env.wait("t", "v")
                    transpose_heads(env, lambda h: sb["evb"][:, h * 128:
                                                            (h + 1) * 128],
                                    "kfb", tb)
                else:
                    rope_blk(env, "kraw", tb, p0, cn, sn, "ktb")
            if own:
                proj_token_major(env, "gWqB", "kraw")
                for tb in range(NTB):
                    p0 = g0 + tb * 128
                    rope_blk(env, "kraw", tb, p0, "ropeC", "ropeS", "evb")
                    env.wait("t", "v")
                    transpose_heads(env, lambda h: sb["evb"][:, h * 128:
                                                            (h + 1) * 128],
                                    "qfb", tb)
            if own:
                hcnt = {}
                for h in range(H):
                    par = h % 2
                    if h >= 2:
                        env.wait("g", "t", at=hcnt[h - 2])
                    else:
                        env.wait("g", "t")
                    env.dma("g",
                            sb["wtile"][:, par * CB * 128:
                                        (par + 1) * CB * 128].rearrange(
                                "p (c n) -> p c n", n=128),
                            t["gWgB"].ap().rearrange(
                                "(c p) d -> p c d",
                                p=128)[:, :, h * 128:(h + 1) * 128])
                    env.done("g", 16)
                    env.wait("t", "g")
                    env.wait("t", "s")
                    for cb2 in range(CB):
                        env.T.matmul(
                            ps[("pe", "pf")[par]][0:128, 0:QG],
                            sb["wtile"][:, par * CB * 128 + cb2 * 128:
                                        par * CB * 128 + (cb2 + 1) * 128],
                            sb["xnf"][:, cb2 * QG:(cb2 + 1) * QG],
                            start=(cb2 == 0), stop=(cb2 == CB - 1))
                    env.done("t")
                    hcnt[h] = env.count("t")
                    env.wait("s", "t")
                    env.S.activation(sb["sgf"][:, :],
                                     ps[("pe", "pf")[par]][0:128, 0:QG],
                                     AF.Silu)
                    env.done("s")
                    env.wait("y", "s")
                    env.dma("y",
                            t["SGsc"].ap()[h, (g0 - S0) // 128:
                                           (g0 - S0) // 128 + NTB].rearrange(
                                "b p q -> p b q"),
                            sb["sgf"][:, :].rearrange(
                                "p (b q) -> p b q", b=NTB))
                    env.done("y", 16)
                    env.wait("s", "y")
            if not own:
                env.wait("t", "v")
                for h in range(H):
                    for sc in range(NTB):
                        env.T.matmul(
                            ps["pe"][0:128, 0:128],
                            sb["ktb"][:, sc * D + h * 128:
                                      sc * D + (h + 1) * 128],
                            sb["vtb"][:, sc * D + h * 128:
                                      sc * D + (h + 1) * 128],
                            start=(sc == 0), stop=(sc == NTB - 1))
                    env.done("t")
                    env.wait("v", "t")
                    env.V.scalar_tensor_tensor(
                        sb["S_st"][:, h * 128:(h + 1) * 128],
                        sb["S_st"][:, h * 128:(h + 1) * 128],
                        float(gSG[h]), ps["pe"][0:128, 0:128],
                        ALU.mult, ALU.add)
                    env.done("v")
                    env.wait("t", "v")
            else:
                ob = ("pe", "pd")
                wb = ("pf", "pc")
                sbf_ = ("S_bf", "S_bf2")
                cnt_o = {}
                cnt_g = {}
                group_v0 = env.count("v")
                env.wait("t", "s")
                env.wait("t", "v")
                for h in range(H):
                    hb = h % 2
                    obk, wbk = ps[ob[hb]], ps[wb[hb]]
                    sbfn = sbf_[hb]
                    if h >= 2:
                        env.wait("t", "s", at=cnt_g[h - 2][0])
                        env.wait("t", "v", at=cnt_g[h - 2][1])
                    env.wait("s", "v", at=group_v0)
                    cSTT = None
                    for ck in range(NTB):
                        env.S.activation(sb[sbfn][:, :],
                                         sb["S_st"][:, h * 128:(h + 1) * 128],
                                         AF.Identity)
                        env.done("s")
                        cS = env.count("s")
                        if cSTT is not None:
                            env.wait("t", "v", at=cSTT)
                        env.T.matmul(wbk[0:128, (ck % 2) * 384:
                                         (ck % 2) * 384 + 128],
                                     sb["kfb"][:, h * QG + ck * 128:
                                               h * QG + (ck + 1) * 128],
                                     sb["qfb"][:, h * QG + ck * 128:
                                               h * QG + (ck + 1) * 128],
                                     start=True, stop=True)
                        env.done("t")
                        cA = env.count("t")
                        env.wait("v", "t", at=cA)
                        eoff = (hb * 2 + ck % 2) * 256
                        env.V.tensor_tensor(
                            sb["evb"][:, eoff:eoff + 128],
                            wbk[0:128, (ck % 2) * 384:(ck % 2) * 384 + 128],
                            sb["dmatT"][:, h * CK:(h + 1) * CK], ALU.mult)
                        env.V.tensor_tensor(
                            sb["evb"][:, eoff + 128:eoff + 256],
                            sb["qfb"][:, h * QG + ck * 128:
                                      h * QG + (ck + 1) * 128],
                            sb["crossC"][:, h * CK:(h + 1) * CK], ALU.mult)
                        env.done("v")
                        cM = env.count("v")
                        env.wait("t", "v", at=cM)
                        env.wait("t", "s", at=cS)
                        env.T.matmul(obk[0:128, ck * 128:(ck + 1) * 128],
                                     sb["vtb"][:, ck * D + h * 128:
                                               ck * D + (h + 1) * 128],
                                     sb["evb"][:, eoff:eoff + 128],
                                     start=True, stop=False)
                        env.T.matmul(obk[0:128, ck * 128:(ck + 1) * 128],
                                     sb[sbfn][:, :],
                                     sb["evb"][:, eoff + 128:eoff + 256],
                                     start=False, stop=True)
                        env.T.matmul(wbk[0:128, 128:256],
                                     sb["ktb"][:, ck * D + h * 128:
                                               ck * D + (h + 1) * 128],
                                     sb["vtb"][:, ck * D + h * 128:
                                               ck * D + (h + 1) * 128],
                                     start=True, stop=True)
                        env.done("t")
                        cU = env.count("t")
                        env.wait("v", "t", at=cU)
                        env.V.scalar_tensor_tensor(
                            sb["S_st"][:, h * 128:(h + 1) * 128],
                            sb["S_st"][:, h * 128:(h + 1) * 128],
                            float(gCK[h]), wbk[0:128, 128:256],
                            ALU.mult, ALU.add)
                        env.done("v")
                        cSTT = env.count("v")
                        if ck + 1 < NTB:
                            env.wait("s", "v", at=cSTT)
                    cnt_o[h] = env.count("t")
                    env.wait("s", "t", at=cnt_o[h])
                    env.S.activation(sb["gsq"][:, 0:QG],
                                     obk[0:128, 0:QG], AF.Square)
                    env.done("s")
                    c1 = env.count("s")
                    env.wait("t", "s", at=c1)
                    env.wait("t", "v", at=cSTT)
                    env.T.matmul(wbk[0:1, 0:QG], sb["onesbf"][:, 0:1],
                                 sb["gsq"][:, 0:QG], start=True, stop=True)
                    env.done("t")
                    c2 = env.count("t")
                    env.wait("s", "t", at=c2)
                    env.S.activation(sb["srow"][0:1, 0:QG],
                                     wbk[0:1, 0:QG], AF.Ln,
                                     scale=1.0 / DH, bias=cfg.EPS_GN)
                    env.done("s")
                    env.wait("s", "s")
                    env.S.activation(sb["srow"][0:1, 0:QG],
                                     sb["srow"][0:1, 0:QG], AF.Exp,
                                     scale=-0.5)
                    env.done("s")
                    c3 = env.count("s")
                    env.wait("t", "s", at=c3)
                    env.T.matmul(wbk[0:128, 0:QG], sb["onesfr"][0:1, :],
                                 sb["srow"][0:1, 0:QG], start=True, stop=True)
                    env.done("t")
                    c4 = env.count("t")
                    env.wait("s", "t", at=c4)
                    if h > 0:
                        env.wait("s", "v", at=cnt_g[h - 1][1])
                    env.S.activation(sb["evac"][:, 0:QG], wbk[0:128, 0:QG],
                                     AF.Identity, scale=sb["gnw"][:, 0:1])
                    env.done("s")
                    c5 = env.count("s")
                    if h > 0:
                        env.wait("g", "v", at=cnt_g[h - 1][1])
                    env.wait("g", "y")
                    env.dma("g", 
                        sb["sgf"][:, :].rearrange("p (b q) -> p b q", b=NTB),
                        t["SGsc"].ap()[h, (g0 - S0) // 128:
                                       (g0 - S0) // 128 + NTB].rearrange(
                            "b p q -> p b q"))
                    env.done("g", 16)
                    env.wait("v", "s", at=c5)
                    env.wait("v", "g")
                    env.V.tensor_tensor(sb["sq32"][:, 0:QG],
                                        obk[0:128, 0:QG],
                                        sb["evac"][:, 0:QG], ALU.mult)
                    env.done("v")
                    env.wait("v", "v")
                    gtoff = 1024 + hb * QG
                    env.V.tensor_tensor(sb["evb"][:, gtoff:gtoff + QG],
                                        sb["sq32"][:, 0:QG],
                                        sb["sgf"][:, :], ALU.mult)
                    env.done("v")
                    c6 = env.count("v")
                    env.wait("y", "v", at=c6)
                    env.dma("y",
                            t["GTsc"].ap()[h, (g0 - S0) // 128:
                                           (g0 - S0) // 128 + NTB].rearrange(
                                "b p q -> p b q"),
                            sb["evb"][:, gtoff:gtoff + QG].rearrange(
                                "p (b q) -> p b q", b=NTB))
                    env.done("y", 16)
                    cnt_g[h] = (c5, c6)
        mk.sb["xf"] = xbufs[0]
        mk.sb["xnf"] = xbufs[0]

    if upto >= 4:
        emit_phase(mk, "ph4", ph_gla, locals_={
            "xf": ((128, CB * QG), BF16),
            "xf2": ((128, CB * QG), BF16),
            "kraw": ((128, NTB * D), BF16),
            "ktb": ((128, NTB * D), BF16),
            "vtb": ((128, NTB * D), BF16),
            "qfb": ((128, H * QG), BF16),
            "kfb": ((128, H * QG), BF16),
            "evb": ((128, max(D, 2048)), BF16),
            "wrhs": ((128, 2 * min(8, CB) * DG), BF16),
            "sgf": ((128, QG), BF16),
            "gsq": ((128, QG), BF16),
            "tta": ((128, RW), F32), "ttb": ((128, RW), F32),
            "rpa": ((128, RW), F32), "rpb": ((128, RW), F32),
            "S_st": ((128, H * 128), F32),
            "S_bf": ((128, 128), BF16),
            "S_bf2": ((128, 128), BF16),
            "kdec128": ((128, D), F32),
            "crossC": ((128, H * CK), F32),
            "dmatT": ((CK, H * CK), F32),
            "rbc": ((128, QG), F32), "mbc": ((128, QG), F32)},
            aliases={"xnf": "xf", "sq32": "rpa", "evac": "rpb",
                     "wtile": "wrhs"})

    def ph_kvm(env):
        c = env.core
        E, S0 = end(c), start(c)
        env.G.memset(sb["stg"][:, :], 0.0)
        env.done("g")
        env.wait("g", "g")
        for h0 in range(H):
            env.G.memset(sb["stg"][:, h0 * 48:h0 * 48 + 1], 1.0)
        env.done("g")
        xbufs = (mk.sb["xf"], mk.sb["xf2"])
        x1bufs = (mk.sb["x1f"], mk.sb["x1f2"])

        def bind(gi):
            mk.sb["xf"] = xbufs[gi % 2]
            mk.sb["xnf"] = xbufs[gi % 2]
            mk.sb["x1f"] = x1bufs[gi % 2]

        def prep(gi, g0):
            bind(gi)
            load_xf_group(env, g0)
            build_bcasts(env, g0, rr=True)
            build_xn(env, with_x1=True)

        groups = list(range(0, E, QG))
        prep(0, 0)
        for gi, g0 in enumerate(groups):
            own = g0 >= S0
            nb0 = g0 // 128
            bind(gi)
            env.wait("t", "v")
            wlist = [("fWkTB", "Ksc", True)]
            if own:
                wlist.append(("fWqTB", "Qsc", False))
            for wname, scname, is_k in wlist:
                hcnt = {}
                for h in range(H):
                    par = h % 2
                    if h >= 2:
                        env.wait("g", "t", at=hcnt[h - 2])
                    else:
                        env.wait("g", "t")
                    env.dma("g",
                            sb["wtile"][:, par * CB * 128:
                                        (par + 1) * CB * 128].rearrange(
                                "p (c n) -> p c n", n=128),
                            t[wname].ap().rearrange(
                                "(c p) d -> p c d",
                                p=128)[:, :, h * 128:(h + 1) * 128])
                    env.done("g", 16)
                    env.wait("t", "g")
                    env.wait("t", "v")
                    env.wait("t", "s")
                    for cb2 in range(CB):
                        env.T.matmul(
                            ps["pa"][0:128, 0:QG],
                            sb["wtile"][:, par * CB * 128 + cb2 * 128:
                                        par * CB * 128 + (cb2 + 1) * 128],
                            sb["x1f"][:, cb2 * QG:(cb2 + 1) * QG],
                            start=(cb2 == 0), stop=(cb2 == CB - 1))
                    env.done("t")
                    hcnt[h] = env.count("t")
                    env.wait("v", "t")
                    env.V.tensor_copy(sb["evb"][:, 0:QG],
                                      ps["pa"][0:128, 0:QG])
                    env.done("v")
                    env.wait("y", "v")
                    base = nb0 if is_k else (g0 - S0) // 128
                    env.dma("y",
                            t[scname].ap()[h, base:base + NTB].rearrange(
                                "b p q -> p b q"),
                            sb["evb"][:, 0:QG].rearrange(
                                "p (b q) -> p b q", b=NTB))
                    env.done("y", 16)
                    if True:
                        env.wait("s", "v")
                        env.S.activation(sb["nsq"][:, 0:QG],
                                         sb["evb"][:, 0:QG], AF.Square)
                        env.done("s")
                        env.wait("t", "s")
                        env.T.matmul(ps["pb"][0:1, 0:QG],
                                     sb["onesbf"][:, 0:1],
                                     sb["nsq"][:, 0:QG],
                                     start=True, stop=True)
                        env.done("t")
                        env.wait("v", "t")
                        if is_k:
                            env.V.reduce_max(sb["srow"][0:1, 0:1],
                                             ps["pb"][0:1, 0:QG],
                                             axis=mybir.AxisListType.X)
                            env.done("v")
                            env.wait("v", "v")
                            env.V.tensor_max(sb["knorm"][0:1, h:h + 1],
                                             sb["knorm"][0:1, h:h + 1],
                                             sb["srow"][0:1, 0:1])
                            env.done("v")
                        else:
                            qi = (g0 - S0) // QG
                            env.V.reduce_max(
                                sb["qnorm"][0:1, qi * 16 + h:qi * 16 + h + 1],
                                ps["pb"][0:1, 0:QG], axis=mybir.AxisListType.X)
                            env.done("v")
                        env.wait("t", "v")
                    env.wait("v", "y")
            if gi + 1 < len(groups):
                prep(gi + 1, groups[gi + 1])
                bind(gi)
            for tb in range(NTB):
                env.wait("t", "v")
                for cb2 in range(CB):
                    env.T.matmul(
                        ps["pc"][0:128, 0:H * R],
                        sb["x1f"][:, cb2 * QG + tb * 128:
                                  cb2 * QG + (tb + 1) * 128],
                        sb["WVMr"][:, cb2 * H * R:(cb2 + 1) * H * R],
                        start=(cb2 == 0), stop=(cb2 == CB - 1))
                env.done("t")
                env.wait("v", "t")
                dstap = bass.AP(sb["stg"], 32,
                                [[sb["stg"].shape[1], 128], [48, H], [1, R]])
                env.V.tensor_copy(dstap, ps["pc"][0:128, 0:H * R])
                env.done("v")
                env.wait("y", "v")
                env.dma("y", t["VMsc"].ap()[nb0 + tb], sb["stg"][:, :])
                env.done("y", 16)
                env.wait("t", "y")
                env.wait("v", "y")
        mk.sb["xf"] = xbufs[0]
        mk.sb["xnf"] = xbufs[0]
        mk.sb["x1f"] = x1bufs[0]

    if upto >= 5:
        emit_phase(mk, "ph5", ph_kvm, locals_={
            "xf": ((128, CB * QG), BF16),
            "xf2": ((128, CB * QG), BF16),
            "x1f": ((128, CB * QG), BF16),
            "x1f2": ((128, CB * QG), BF16),
            "wtile": ((128, 2 * CB * 128), BF16),
            "evb": ((128, QG), BF16),
            "stg": ((128, H * 48), BF16),
            "nsq": ((128, QG), BF16),
            "rbc": ((128, QG), F32), "mbc": ((128, QG), F32),
            "rrbc": ((128, QG), F32)},
            aliases={"xnf": "xf"})

    def ph_attn(env):
        c = env.core
        S0 = start(c)
        NQ = n_own(c) // QG
        env.dma("g", sb["masks"][:, :], t["masks"].ap())
        env.done("g", 16)
        for h in range(H):
            for qi in range(NQ):
                env.wait("g", "t")
                env.dma("g",
                        sb["Qtile"][:, :].rearrange("p (b q) -> p b q",
                                                    b=NTB),
                        t["Qsc"].ap()[h, qi * NTB:(qi + 1) * NTB].rearrange(
                            "b p q -> p b q"))
                env.done("g", 16)
                env.wait("v", "g")
                env.V.tensor_tensor(
                    sb["srow"][0:1, 0:1],
                    sb["qnorm"][0:1, qi * 16 + h:qi * 16 + h + 1],
                    sb["knorm"][0:1, h:h + 1], ALU.mult)
                env.done("v")
                env.wait("s", "v")
                env.S.activation(sb["srow"][0:1, 1:2], sb["srow"][0:1, 0:1],
                                 AF.Ln)
                env.done("s")
                env.wait("s", "s")
                env.S.activation(sb["srow"][0:1, 1:2], sb["srow"][0:1, 1:2],
                                 AF.Exp, scale=0.5)
                env.done("s")
                env.wait("v", "s")
                env.V.tensor_scalar_mul(sb["srow"][0:1, 2:3],
                                        sb["srow"][0:1, 1:2], -ISQ)
                env.done("v")
                env.wait("t", "v")
                env.T.matmul(ps["pd"][0:128, 0:1], sb["onesfr"][0:1, :],
                             sb["srow"][0:1, 2:3], start=True, stop=True)
                env.done("t")
                env.wait("v", "t")
                env.V.tensor_copy(sb["bias"][:, 0:1], ps["pd"][0:128, 0:1])
                env.done("v")
                nsb = (S0 + (qi + 1) * QG) // 128
                KB4 = 4
                t_scores = {}
                s_exp = {}
                ksrc = t["Ksc"].ap()[h]
                for si in range(nsb):
                    par = si % 2
                    bpar = (si // KB4) % 2
                    if si % KB4 == 0:
                        nb_ = min(KB4, nsb - si)
                        if si >= 2 * KB4:
                            env.wait("g", "t",
                                     at=t_scores[si - KB4] +
                                     (2 if si - KB4 > 0 else 1))
                        env.dma("g",
                                sb["Ktile"][:, bpar * KB4 * 128:
                                            bpar * KB4 * 128 + nb_ * 128
                                            ].rearrange(
                                    "p (b n) -> p b n", n=128),
                                ksrc[si:si + nb_].rearrange("b p n -> p b n"))
                        env.done("g", 16)
                        env.dma("g",
                                sb["VMtile"][:, bpar * KB4 * 48:
                                             bpar * KB4 * 48 + nb_ * 48
                                             ].rearrange(
                                    "p (b n) -> p b n", n=48),
                                t["VMsc"].ap()[si:si + nb_, :,
                                               h * 48:(h + 1) * 48
                                               ].rearrange("b p n -> p b n"))
                        env.done("g", 16)
                        env.wait("t", "g")
                    if si == 0:
                        env.wait("t", "v")
                    pbank = ps["pa"] if par == 0 else ps["pb"]
                    if si >= 2:
                        env.wait("t", "s", at=s_exp[si - 2])
                    koff = (bpar * KB4 + si % KB4) * 128
                    t_scores[si] = env.count("t")
                    env.T.matmul(pbank[0:128, 0:QG],
                                 sb["Ktile"][:, koff:koff + 128],
                                 sb["Qtile"][:, :], start=True, stop=True)
                    env.done("t")
                    if si > 0:
                        env.wait("t", "s", at=s_exp[si - 1])
                        vmo = (((si - 1) // KB4) % 2) * KB4 * 48 + \
                            ((si - 1) % KB4) * 48
                        env.T.matmul(
                            ps["pc"][0:48, 0:QG],
                            sb["VMtile"][:, vmo:vmo + 48],
                            sb["ptile"][:, ((si - 1) % 2) * QG:
                                        ((si - 1) % 2 + 1) * QG],
                            start=(si - 1 == 0), stop=False)
                        env.done("t")
                    dspos = si - (S0 + qi * QG) // 128
                    if dspos >= 0:
                        env.wait("v", "t", at=t_scores[si] + 1)
                        env.V.tensor_tensor(
                            pbank[0:128, 0:QG], pbank[0:128, 0:QG],
                            sb["masks"][:, dspos * QG:(dspos + 1) * QG],
                            ALU.add)
                        env.done("v")
                        env.wait("s", "v")
                    else:
                        env.wait("s", "t", at=t_scores[si] + 1)
                    env.S.activation(sb["ptile"][:, par * QG:(par + 1) * QG],
                                     pbank[0:128, 0:QG], AF.Exp, scale=ISQ,
                                     bias=sb["bias"][:, 0:1])
                    env.done("s")
                    s_exp[si] = env.count("s")
                si = nsb - 1
                env.wait("t", "s", at=s_exp[si])
                vmo = ((si // KB4) % 2) * KB4 * 48 + (si % KB4) * 48
                env.T.matmul(ps["pc"][0:48, 0:QG],
                             sb["VMtile"][:, vmo:vmo + 48],
                             sb["ptile"][:, (si % 2) * QG:
                                         (si % 2 + 1) * QG],
                             start=(si == 0), stop=True)
                env.done("t")
                env.wait("v", "t")
                env.V.reciprocal(sb["srow"][0:1, 0:QG], ps["pc"][0:1, 0:QG])
                env.done("v")
                env.wait("s", "t")
                env.wait("s", "v")
                env.S.activation(sb["m16"][:, :], ps["pc"][32:48, 0:QG],
                                 AF.Identity)
                env.done("s")
                env.wait("t", "v")
                env.wait("t", "s")
                env.T.matmul(ps["pd"][0:16, 0:QG], sb["onesfr"][0:1, 0:16],
                             sb["srow"][0:1, 0:QG], start=True, stop=True)
                env.done("t")
                env.wait("v", "s")
                env.wait("v", "t")
                env.V.tensor_tensor(sb["m16"][:, :], sb["m16"][:, :],
                                    ps["pd"][0:16, 0:QG], ALU.mult)
                env.done("v")
                env.wait("v", "v")
                env.V.tensor_add(sb["modacc"][0:R, qi * QG:(qi + 1) * QG],
                                 sb["modacc"][0:R, qi * QG:(qi + 1) * QG],
                                 sb["m16"][:, :])
                env.done("v")
                env.wait("t", "v")
    wvm_stack.close()
    mk.sb.pop("WVMr", None)
    if upto >= 6:
        emit_phase(mk, "ph6", ph_attn, locals_={
            "Ktile": ((128, 8 * 128), BF16),
            "VMtile": ((128, 8 * 48), BF16),
            "Qtile": ((128, QG), BF16),
            "ptile": ((128, 2 * QG), BF16),
            "m16": ((16, QG), F32),
            "masks": ((128, NMV * QG), F32)})

    def ph_mlp_final(env):
        c = env.core
        n = n_own(c)
        env.wait("g", "b")
        S0 = start(c)
        nhalf = (n + MLPH - 1) // MLPH
        for hv in range(nhalf):
            t0 = hv * MLPH
            TL = min(MLPH, n - t0)
            for gg in range(0, TL, QG):
                g0 = S0 + t0 + gg
                load_xf_group(env, g0)
                build_bcasts(env, g0, rr=True)
                build_xn(env)
                env.wait("v", "v")
                for cb in range(CB):
                    sl = slice(cb * QG, (cb + 1) * QG)
                    env.V.scalar_tensor_tensor(
                        sb["hf"][:, cb * MLPH + gg:cb * MLPH + gg + QG],
                        sb["xnf"][:, sl], sb["ln2f"][:, cb:cb + 1],
                        sb["rrbc"][:, :], ALU.mult, ALU.mult)
                env.done("v")
                env.wait("g", "v")
                env.wait("t", "v")
            for gg in range(0, TL, 512):
                w = min(512, TL - gg)
                env.wait("t", "s")
                for cb in range(CB):
                    env.T.matmul(ps["pd"][0:R, 0:w],
                                 sb["A_bf"][:, cb * R:(cb + 1) * R],
                                 sb["hf"][:, cb * MLPH + gg:
                                           cb * MLPH + gg + w],
                                 start=(cb == 0), stop=(cb == CB - 1))
                env.done("t")
                env.wait("s", "t")
                env.S.activation(sb["hAT"][0:R, gg:gg + w],
                                 ps["pd"][0:R, 0:w], AF.Identity,
                                 scale=cfg.MOD_SCALE)
                env.done("s")
            env.wait("s", "s")
            env.S.activation(sb["mhT"][0:R, 0:TL],
                             sb["modacc"][0:R, t0:t0 + TL], AF.Tanh)
            env.done("s")
            env.wait("v", "s")
            env.V.tensor_tensor(sb["mhT"][0:R, 0:TL], sb["mhT"][0:R, 0:TL],
                                sb["hAT"][0:R, 0:TL], ALU.mult)
            env.done("v")
            env.wait("t", "v")
            def ld_inup(fi2, at_t):
                if at_t is not None:
                    env.wait("g", "t", at=at_t)
                else:
                    env.wait("g", "t")
                wz = (fi2 % 2) * 2 * CB * 128
                env.dma("g",
                        sb["wtile"][:, wz:wz + CB * 128].rearrange(
                            "p (c n) -> p c n", n=128),
                        t["WinT"].ap().rearrange(
                            "(c p) d -> p c d",
                            p=128)[:, :, fi2 * 128:(fi2 + 1) * 128])
                env.done("g", 16)
                env.dma("g", sb["btile"][0:R, (fi2 % 2) * 128:
                                         (fi2 % 2) * 128 + 128],
                        t["Bmat"].ap()[:, fi2 * 128:(fi2 + 1) * 128])
                env.done("g", 16)
                env.dma("g",
                        sb["wtile"][:, wz + CB * 128:
                                    wz + 2 * CB * 128].rearrange(
                            "p (c n) -> p c n", n=128),
                        t["WupT"].ap().rearrange(
                            "(c p) d -> p c d",
                            p=128)[:, :, fi2 * 128:(fi2 + 1) * 128])
                env.done("g", 16)

            fi_t = {}
            t_in = {}
            t_up = {}
            s_cnt = {}
            v_cnt = {}
            jj = 0
            ld_inup(0, None)
            for fg in range(FB // 8):
                for fb in range(8):
                    fi = fg * 8 + fb
                    slot = fi % 2
                    w0 = slot * 2 * CB * 128
                    env.wait("t", "g")
                    if fb == 0:
                        env.wait("t", "v")
                        env.wait("t", "s")
                    for gg in range(0, TL, 512):
                        w = min(512, TL - gg)
                        par = jj % 2
                        bin_ = ps[("pa", "pc")[par]]
                        bup = ps[("pb", "pd")[par]]
                        fg0 = (fb == 0 and gg == 0)
                        if jj >= 2 and not fg0:
                            env.wait("t", "s", at=s_cnt[jj - 2])
                        for cb in range(CB):
                            env.T.matmul(
                                bin_[0:128, 0:w],
                                sb["wtile"][:, w0 + cb * 128:
                                            w0 + (cb + 1) * 128],
                                sb["hf"][:, cb * MLPH + gg:
                                         cb * MLPH + gg + w],
                                start=(cb == 0), stop=False)
                        env.T.matmul(bin_[0:128, 0:w],
                                     sb["btile"][0:R, slot * 128:
                                                 slot * 128 + 128],
                                     sb["mhT"][0:R, gg:gg + w],
                                     start=False, stop=True)
                        env.done("t")
                        t_in[jj] = env.count("t")
                        if jj >= 2 and not fg0:
                            env.wait("t", "v", at=v_cnt[jj - 2])
                        for cb in range(CB):
                            env.T.matmul(
                                bup[0:128, 0:w],
                                sb["wtile"][:, w0 + CB * 128 + cb * 128:
                                            w0 + CB * 128 + (cb + 1) * 128],
                                sb["hf"][:, cb * MLPH + gg:
                                         cb * MLPH + gg + w],
                                start=(cb == 0), stop=(cb == CB - 1))
                        env.done("t")
                        t_up[jj] = env.count("t")
                        env.wait("s", "t", at=t_in[jj])
                        if jj >= 2:
                            env.wait("s", "v", at=v_cnt[jj - 2])
                        env.S.activation(sb["sgf"][:, par * QG:par * QG + w],
                                         bin_[0:128, 0:w], AF.Silu)
                        env.done("s")
                        s_cnt[jj] = env.count("s")
                        env.wait("v", "t", at=t_up[jj])
                        env.wait("v", "s", at=s_cnt[jj])
                        env.V.tensor_tensor(
                            sb["gustg"][:, fb * MLPH + gg:
                                        fb * MLPH + gg + w],
                            bup[0:128, 0:w],
                            sb["sgf"][:, par * QG:par * QG + w],
                            ALU.mult)
                        env.done("v")
                        v_cnt[jj] = env.count("v")
                        jj += 1
                    fi_t[fi] = env.count("t")
                    if fi + 1 < FB:
                        ld_inup(fi + 1, fi_t[fi - 1] if fi >= 1 else None)
                for hb8 in range(2):
                    env.wait("g", "t")
                    for fb in range(hb8 * 4, hb8 * 4 + 4):
                        fi2b = fg * 8 + fb
                        env.dma("g", sb["wdnb"][:, (fb % 4) * D:
                                                (fb % 4 + 1) * D],
                                t["WdnT"].ap()[fi2b * 128:
                                               (fi2b + 1) * 128, :])
                        env.done("g", 16)
                    env.wait("t", "g")
                    env.wait("t", "v")
                    wd_t = {}
                    wd_v = {}
                    wd_i = 0
                    for tb in range(TL // 128):
                        for dg in range(D // DG):
                            bank = ("pc", "pd")[wd_i % 2]
                            if wd_i >= 2:
                                env.wait("t", "v", at=wd_v[wd_i - 2])
                            for fb in range(4):
                                env.T.matmul(
                                    ps[bank][0:128, 0:DG],
                                    sb["gustg"][:, (hb8 * 4 + fb) * MLPH +
                                                tb * 128:
                                                (hb8 * 4 + fb) * MLPH +
                                                (tb + 1) * 128],
                                    sb["wdnb"][:, fb * D + dg * DG:
                                               fb * D + (dg + 1) * DG],
                                    start=(fb == 0), stop=(fb == 3))
                            env.done("t")
                            wd_t[wd_i] = env.count("t")
                            env.wait("v", "t", at=wd_t[wd_i])
                            dst = sb["mlp"][:, tb * D + dg * DG:
                                            tb * D + (dg + 1) * DG]
                            if fg == 0 and hb8 == 0:
                                env.V.tensor_copy(dst, ps[bank][0:128, 0:DG])
                            else:
                                env.V.tensor_add(dst, dst,
                                                 ps[bank][0:128, 0:DG])
                            env.done("v")
                            wd_v[wd_i] = env.count("v")
                            wd_i += 1
            for dg in range(D // DG):
                env.wait("g", "t")
                env.dma("g",
                        sb["wdnb"][:, 0:CB * DG].rearrange(
                            "p (c n) -> p c n", n=DG),
                        t["gWo"].ap().rearrange(
                            "(c p) d -> p c d",
                            p=128)[:, :, dg * DG:(dg + 1) * DG])
                env.done("g", 16)
                for tb in range(TL // 128):
                    tg = t0 // 128 + tb
                    env.wait("g", "t")
                    env.wait("g", "v")
                    env.wait("g", "s")
                    env.dma("g",
                            sb["gtile"][:, 0:H * 128].rearrange(
                                "p (c n) -> p c n", n=128),
                            t["GTsc"].ap()[:, tg].rearrange("c p n -> p c n"))
                    env.done("g", 16)
                    env.dma("g", 
                        sb["xtile"][:, :],
                        t["x"].ap()[S0 + t0 + tb * 128:
                                    S0 + t0 + (tb + 1) * 128,
                                    dg * DG:(dg + 1) * DG])
                    env.done("g", 16)
                    env.wait("t", "g")
                    env.wait("t", "v")
                    for h in range(H):
                        env.T.matmul(ps["pa"][0:128, 0:DG],
                                     sb["gtile"][:, h * 128:(h + 1) * 128],
                                     sb["wdnb"][:, h * DG:(h + 1) * DG],
                                     start=(h == 0), stop=(h == H - 1))
                    env.done("t")
                    env.wait("s", "g")
                    env.wait("s", "v")
                    env.wait("s", "y")
                    env.S.activation(sb["ftile"][:, :], sb["xtile"][:, :],
                                     AF.Identity,
                                     scale=sb["r_col"][:, tg:tg + 1],
                                     bias=sb["nmh_col"][:, tg:tg + 1])
                    env.done("s")
                    env.wait("v", "s")
                    env.V.tensor_add(sb["ftile"][:, :], sb["ftile"][:, :],
                                     sb["mlp"][:, tb * D + dg * DG:
                                               tb * D + (dg + 1) * DG])
                    env.done("v")
                    env.wait("s", "v")
                    env.S.activation(sb["ftile"][:, :], sb["ftile"][:, :],
                                     AF.Silu)
                    env.done("s")
                    env.wait("v", "s")
                    env.wait("v", "t")
                    env.V.tensor_tensor(sb["ftile"][:, :],
                                        ps["pa"][0:128, 0:DG],
                                        sb["ftile"][:, :], ALU.mult)
                    env.done("v")
                    env.wait("v", "v")
                    env.V.tensor_add(sb["ftile"][:, :], sb["ftile"][:, :],
                                     sb["xtile"][:, :])
                    env.done("v")
                    env.wait("y", "v")
                    env.dma("y", 
                        t["out"].ap()[t0 + tb * 128:t0 + (tb + 1) * 128,
                                      dg * DG:(dg + 1) * DG],
                        sb["ftile"][:, :])
                    env.done("y", 16)
                    env.wait("t", "y")
                    env.wait("v", "y")
                    env.wait("g", "s")
        env.wait("y", "y")
    if upto >= 7:
        emit_phase(mk, "ph7", ph_mlp_final, locals_={
            "xf": ((128, CB * QG), BF16),
            "hf": ((128, CB * MLPH), BF16),
            "gustg": ((128, 8 * MLPH), BF16),
            "wtile": ((128, 4 * CB * 128), BF16),
            "wdnb": ((128, 4 * D), BF16),
            "sgf": ((128, 2 * QG), BF16),
            "xtile": ((128, DG), F32), "ftile": ((128, DG), F32),
            "hAT": ((R, MLPH), BF16), "mhT": ((R, MLPH), BF16),
            "mlp": ((128, (MLPH // 128) * D), F32),
            "rbc": ((128, QG), F32), "mbc": ((128, QG), F32),
            "rrbc": ((128, QG), F32)},
            aliases={"xnf": "xf", "gtile": "wtile"})
    if upto < 7:
        with nc.Block("trailer") as b:
            @b.sync
            def _(y):
                strl = mk.semaphore("trailer_sem")
                y.dma_start(t["out"].ap()[0:1, 0:16],
                            sb["cvec"][0:1, 0:16]).then_inc(strl, 16)
                y.wait_ge(strl, 16)

    return mk


def make_inmaps(cfg, inputs):
    S, D = cfg.S, cfg.D
    tabs = host_tables(cfg)
    x = np.ascontiguousarray(np.asarray(inputs["L0_attn_x"],
                                        np.float32).reshape(S, D))
    cv = np.zeros((128, 16), np.float32)
    for i, v in enumerate([cfg.EPS_LN, cfg.EPS_RMS, cfg.EPS_GN, 1.0, -1.0,
                           1.0 / D, 1.0 / cfg.DH]):
        cv[:, i] = v
    f32 = lambda a: np.ascontiguousarray(np.asarray(a, np.float32))
    base = {
        "x": x,
        "xT": np.ascontiguousarray(x.T),
        "gWq": f32(inputs["L0_attn_Wq"]),
        "gWk": f32(inputs["L0_attn_Wk"]),
        "gWv": f32(inputs["L0_attn_Wv"]),
        "gWg": f32(inputs["L0_attn_Wg"]),
        "gWo": f32(inputs["L0_attn_Wo"]),
        "gnw": f32(inputs["L0_attn_g_norm_w"]).reshape(128, 1),
        "fWqT": np.ascontiguousarray(f32(inputs["L0_ffn_Wq"]).T),
        "fWkT": np.ascontiguousarray(f32(inputs["L0_ffn_Wk"]).T),
        "fWv": f32(inputs["L0_ffn_Wv"]),
        "fWo": f32(inputs["L0_ffn_Wo"]),
        "Amat": f32(inputs["L0_ffn_A"]),
        "Bmat": f32(inputs["L0_ffn_B"]),
        "WinT": np.ascontiguousarray(f32(inputs["L0_ffn_W_in_base"]).T),
        "WupT": np.ascontiguousarray(f32(inputs["L0_ffn_W_up_base"]).T),
        "WdnT": np.ascontiguousarray(f32(inputs["L0_ffn_W_down"]).T),
        "ln1f": np.ascontiguousarray(
            f32(inputs["L0_ffn_ln1_w"]).reshape(cfg.CB, 128).T),
        "ln2f": np.ascontiguousarray(
            f32(inputs["L0_ffn_ln2_w"]).reshape(cfg.CB, 128).T),
        "cvec": cv,
        "ropeC": tabs["ropeC"], "ropeS": tabs["ropeS"],
        "ropeCd": tabs["ropeCd"], "ropeSd": tabs["ropeSd"],
        "kdec128": tabs["kdec128"], "crossC": tabs["crossC"],
        "dmatT": tabs["dmatT"], "masks": tabs["masks"],
        "ident": tabs["ident"],
    }
    in_maps = []
    for c in range(cfg.n_cores):
        m = dict(base)
        m["cid"] = np.array([[c]], np.int32)
        in_maps.append(m)
    return in_maps


def kernel(**inputs):
    cfg = Cfg(splits=[1024, 1024, 1024, 1024, 1024, 1024, 1024, 1024])
    mk = build(cfg)
    in_maps = make_inmaps(cfg, inputs)
    res = run_bass_kernel_spmd(mk.nc, in_maps,
                               core_ids=list(range(cfg.n_cores)))
    outs = [res.results[c]["out"][0:cfg.splits[c]]
            for c in range(cfg.n_cores)]
    full = np.concatenate(outs, axis=0).astype(np.float32)
    return full.reshape(1, cfg.S, cfg.D)
